# version 02
# baseline (speedup 1.0000x reference)
import math
import contextlib
import numpy as np
import ml_dtypes
import concourse.bass as bass
import concourse.mybir as mybir
from concourse.bass_utils import run_bass_kernel_spmd

F32 = mybir.dt.float32
BF16 = mybir.dt.bfloat16
I32 = mybir.dt.int32
AF = mybir.ActivationFunctionType
ALU = mybir.AluOpType
AX = mybir.AxisListType.X
E_SP = mybir.EngineType.SP

BF = ml_dtypes.bfloat16

D = 4096
S = 8192
NCORES = 8
TPC = 1024
TT = 512
NTT = 2
NCH = 32
HD = 128
KVH = 8
NSLOT = 4
NQG = 16
SCALE = 1.0 / math.sqrt(128.0)
SQHD = math.sqrt(128.0)
EPS = 1e-5
WB_A = (1, 3, 9, NQG)
WB_B = (NQG, NQG, NQG, NQG)
PHASES = 4
SUBSTAGE = 9

DT_SZ = {F32: 4, BF16: 2, I32: 4}


def build_program():
    nc = bass.Bass("TRN2", target_bir_lowering=False)

    def din(name, shape, dt):
        return nc.dram_tensor(name, shape, dt, kind="ExternalInput").ap()

    x0 = din("x0", [D, TPC], F32)
    wq_t = din("wq_t", [128, 32 * 3 * D], F32)
    bqkv = din("bqkv", [128, 96], F32)
    wo_t = din("wo_t", [128, 32 * D], F32)
    bout = din("bout", [128, 32], F32)
    wb_t = din("wb_t", [128, 32 * D], F32)
    bbase = din("bbase", [128, 32], F32)
    wr_t = din("wr_t", [D, 8], F32)
    wa_t = din("wa_t", [128, 32 * 128], F32)
    wb2 = din("wb2", [128, D], F32)
    qcod = din("qcod", [128, 32 * D], I32)
    kcod = din("kcod", [128, 32 * KVH * HD], I32)
    vcod = din("vcod", [128, 32 * KVH * HD], I32)
    ocod = din("ocod", [128, 32 * D], I32)
    qscl = din("qscl", [32, D], F32)
    kscl = din("kscl", [32, KVH * HD], F32)
    vscl = din("vscl", [32, KVH * HD], F32)
    oscl = din("oscl", [32, D], F32)
    ropeA = din("ropeA", [128, TPC], BF16)
    ropeB = din("ropeB", [128, TPC], BF16)
    ident = din("ident", [128, 128], BF16)
    identf = din("identf", [128, 128], F32)
    onesr = din("onesr", [1, 128], BF16)
    onesrf = din("onesrf", [1, 128], F32)
    onesc = din("onesc", [128, 1], BF16)
    ones512 = din("ones512", [1, TT], BF16)
    tri = din("tri", [128, 4 * TT], BF16)
    epsap = din("epsap", [1, 1], F32)
    iota128f = din("iota128f", [1, 128], F32)
    sloper = din("sloper", [1, 256], F32)
    baser = din("baser", [1, 256], F32)
    negaq = din("negaq", [4, S], BF16)
    expand = din("expand", [8, 128], BF16)

    xout = nc.dram_tensor("xout", [D, TPC], F32, kind="ExternalOutput").ap()

    def dint(name, shape, dt, shared=False):
        return nc.dram_tensor(name, shape, dt,
                              addr_space=("Shared" if shared else "Local")).ap()

    agin_a = dint("agin_a", [3 * D, TPC], BF16)
    agout_a = dint("agout_a", [NCORES * 3 * D, TPC], BF16, shared=True)
    agin_b = dint("agin_b", [D + 2 * KVH * HD, TPC], BF16)
    agout_b = dint("agout_b", [NCORES * (D + 2 * KVH * HD), TPC], BF16, shared=True)
    q_loc = dint("q_loc", [NSLOT * HD, S], BF16)
    k_loc = dint("k_loc", [NSLOT * HD, S], BF16)
    v_loc = dint("v_loc", [S, NSLOT * HD], BF16)
    q_locb = dint("q_locb", [NSLOT * HD, S], BF16)
    k_locb = dint("k_locb", [HD, S], BF16)
    v_locb = dint("v_locb", [S, HD], BF16)
    ao_loc_a = dint("ao_loc_a", [TPC, D], BF16)
    ao_loc_b = dint("ao_loc_b", [TPC, D], BF16)
    aoin_a = dint("aoin_a", [S, NSLOT * HD], BF16)
    aoout_a = dint("aoout_a", [NCORES * S, NSLOT * HD], BF16, shared=True)
    aoin_b = dint("aoin_b", [S, NSLOT * HD], BF16)
    aoout_b = dint("aoout_b", [NCORES * S, NSLOT * HD], BF16, shared=True)

    SBUF_BASE = 16512
    SBUF_LIMIT = 229056
    _off = [SBUF_BASE]

    def place(name, shape, dt):
        per = DT_SZ[dt]
        for s_ in shape[1:]:
            per *= s_
        t = nc.alloc_sbuf_tensor_at(name, shape, dt, offset=_off[0])
        _off[0] += (per + 31) // 32 * 32
        assert _off[0] <= SBUF_LIMIT, f"sbuf overflow at {name}: {_off[0]}"
        return t

    x_sb = place("x_sb", [128, NCH * TPC], F32)
    xn_sb = place("xn_sb", [128, NCH * TT], BF16)
    c_ident = place("c_ident", [128, 128], BF16)
    c_identf = place("c_identf", [128, 128], F32)
    c_onesr = place("c_onesr", [1, 128], BF16)
    c_onesrf = place("c_onesrf", [1, 128], F32)
    c_onesc = place("c_onesc", [128, 1], BF16)
    c_bqkv = place("c_bqkv", [128, 96], F32)
    c_bout = place("c_bout", [128, 32], F32)
    c_bbase = place("c_bbase", [128, 32], F32)
    c_eps = place("c_eps", [1, 1], F32)
    c_expand = place("c_expand", [8, 128], BF16)
    c_neg4 = place("c_neg4", [128, 1], F32)
    ln_xb = [place("ln_xb0", [128, TT], BF16)]
    ln_xq = [place("ln_xq0", [128, TT], BF16)]
    ln_rows = place("ln_rows", [1, 2 * TT], F32)
    ln_tapp = place("ln_tapp", [128, TT], F32)
    SCR0 = _off[0]

    class Arena:
        def __init__(self, tag, base=SCR0):
            self.tag = tag
            self.off = base

        def get(self, name, shape, dt):
            per = DT_SZ[dt]
            for s_ in shape[1:]:
                per *= s_
            end = self.off + (per + 31) // 32 * 32
            assert end <= SBUF_LIMIT, f"sbuf overflow {self.tag}{name}: {end}"
            t = nc.alloc_sbuf_tensor_at(self.tag + name, shape, dt, offset=self.off)
            self.off = end
            return t

    PB = [nc.alloc_psum_tensor(f"pb{i}", [128, 512], F32) for i in range(8)]

    def pb_bf(i):
        return PB[i].ap().bitcast(BF16)

    stack = contextlib.ExitStack()
    SEMNAMES = ["ini", "w0", "w1", "cvt", "pe", "ve", "se",
                "kv0", "kv1", "qp0", "qp1", "sto", "ext", "cc", "fin",
                "st0", "st1", "vsq", "ao0", "ao1", "sr"]
    SEM = {n: stack.enter_context(nc.semaphore(n)) for n in SEMNAMES}
    CNT = {n: 0 for n in SEMNAMES}
    ENGS = ("sync", "scalar", "vector", "tensor", "gpsimd")
    PLAN = {e: [] for e in ENGS}

    _SELF = {"vector": "ve", "scalar": "se", "gpsimd": "cvt"}

    def E(eng, meth, *args, w=(), i=(), lazy=None, **kw):
        w = list(w)
        if meth is not None and eng in _SELF:
            w.append((_SELF[eng], CNT[_SELF[eng]]))
        waits = [(SEM[s], v) for (s, v) in w if v > 0]
        assert not (meth is None and i), "inc on a None op never fires"
        incs = []
        for (s_, n_) in i:
            CNT[s_] += n_
            incs.append((SEM[s_], n_))

        def run(h, _m=meth, _a=args, _k=kw, _w=waits, _i=incs, _lz=lazy):
            for (sem, val) in _w:
                h.wait_ge(sem, val)
            if _m is None:
                return
            a = list(_a)
            k2 = dict(_k)
            if _lz is not None:
                k2["in_"] = _lz(h)[0]
            inst = getattr(h, _m)(*a, **k2)
            for (sem, n_) in _i:
                inst = inst.then_inc(sem, n_)
        PLAN[eng].append(run)

    def cur(s):
        return (s, CNT[s])

    def t_mm(out, lhsT, rhs, start, stop, w=(), i=()):
        E("tensor", "matmul", out, lhsT, rhs, start=start, stop=stop, w=w, i=i)

    def barrier():
        snap = [cur(s) for s in ("pe", "ve", "se", "cvt", "sto", "ext", "cc",
                                 "w0", "w1", "kv0", "kv1", "qp0", "qp1")]
        for eng in ENGS:
            E(eng, None, w=snap)

    E("sync", "dma_start", out=x_sb[:].rearrange("p (c t) -> p c t", t=TPC),
      in_=x0.rearrange("(c p) t -> p c t", p=128), i=[("ini", 16)])
    for (dst, src) in [(c_ident, ident), (c_identf, identf), (c_onesr, onesr),
                       (c_onesrf, onesrf), (c_onesc, onesc),
                       (c_bqkv, bqkv), (c_bout, bout), (c_bbase, bbase),
                       (c_eps, epsap), (c_expand, expand)]:
        E("sync", "dma_start", out=dst[:], in_=src, i=[("ini", 16)])
    for eng in ENGS:
        E(eng, None, w=[cur("ini")])
    E("vector", "memset", c_neg4[:], -4.0, i=[("ve", 1)])


    def ln_into_xn(tt):
        for c in range(NCH):
            xc = x_sb[:, c * TPC + tt * TT: c * TPC + tt * TT + TT]
            wv = [cur("pe")] if c >= 1 else []
            E("vector", "tensor_copy", out=ln_xb[0][:], in_=xc, w=wv, i=[("ve", 1)])
            E("scalar", "activation", out=ln_xq[0][:], in_=xc, func=AF.Square,
              w=wv, i=[("se", 1)])
            t_mm(PB[2][:1, :], c_onesc[:], ln_xb[0][:], start=(c == 0),
                 stop=(c == NCH - 1), w=[cur("ve"), cur("se")], i=[("pe", 1)])
            t_mm(PB[3][:1, :], c_onesc[:], ln_xq[0][:], start=(c == 0),
                 stop=(c == NCH - 1), i=[("pe", 1)])
        r_m = ln_rows[:, 0:TT]
        r_t = ln_rows[:, TT:2 * TT]
        E("vector", "tensor_scalar", w=[cur("pe")], i=[("ve", 1)],
          out=r_m, in0=PB[2][:1, :], scalar1=1.0 / D, scalar2=None, op0=ALU.mult)
        E("vector", "tensor_scalar", i=[("ve", 1)],
          out=r_t, in0=PB[3][:1, :], scalar1=1.0 / D, scalar2=None, op0=ALU.mult)
        E("vector", "scalar_tensor_tensor", out=r_t, in0=r_m, scalar=-1.0,
          in1=r_m, op0=ALU.mult, op1=ALU.mult, i=[("ve", 1)], accum_out=None)
        E("vector", "tensor_scalar", i=[("ve", 1)],
          out=ln_tapp[:1, 0:TT], in0=PB[3][:1, :], scalar1=1.0 / D, scalar2=None,
          op0=ALU.mult)
        E("vector", "tensor_add", out=r_t, in0=r_t, in1=ln_tapp[:1, 0:TT],
          i=[("ve", 1)])
        E("scalar", "activation", out=r_t, in_=r_t, func=AF.Sqrt, bias=c_eps[:],
          w=[cur("ve")], i=[("se", 1)])
        E("vector", "reciprocal", out=r_t, in_=r_t, w=[cur("se")], i=[("ve", 1)])
        E("vector", "scalar_tensor_tensor", out=r_m, in0=r_m, scalar=-1.0,
          in1=r_t, op0=ALU.mult, op1=ALU.mult, i=[("ve", 1)])
        t_mm(PB[2][:, :], c_onesrf[:], r_t, start=True, stop=True,
             w=[cur("ve")], i=[("pe", 1)])
        t_mm(PB[3][:, :], c_onesrf[:], r_m, start=True, stop=True, i=[("pe", 1)])
        for c in range(NCH):
            xc = x_sb[:, c * TPC + tt * TT: c * TPC + tt * TT + TT]
            xnc = xn_sb[:, c * TT:(c + 1) * TT]
            E("vector", "tensor_mul", out=ln_tapp[:], in0=xc, in1=PB[2][:, :],
              w=[cur("pe")] if c == 0 else [], i=[("ve", 1)])
            E("vector", "tensor_add", out=xnc, in0=ln_tapp[:], in1=PB[3][:, :],
              i=[("ve", 1)])
        E("vector", "engine_nop", i=[("ve", 1)])
        return cur("ve")

    def make_wmach(ar, tag):
        m = {
            "wst": [ar.get(tag + "wst0", [128, 1024], F32),
                    ar.get(tag + "wst1", [128, 1024], F32)],
            "wbf": [ar.get(tag + "wbf0", [128, 4096], BF16),
                    ar.get(tag + "wbf1", [128, 4096], BF16)],
            "wst_free": [0, 0],
            "wbf_free": [0, 0],
            "hidx": 0,
        }
        return m

    def wload(m, wsrc, oc, par, kmajor=True, dst=None):
        if dst is None:
            dst = m["wbf"][par]
        for h in range(4):
            b = m["hidx"] % 2
            m["hidx"] += 1
            sem = "w%d" % b
            if kmajor:
                src = wsrc[:, (oc * 4 + h) * 1024:(oc * 4 + h + 1) * 1024]
            else:
                src = wsrc[:, h * 1024:(h + 1) * 1024]
            dst_st = m["wst"][b][:]
            E("scalar", "dma_start", out=dst_st, in_=src,
              w=[("cvt", m["wst_free"][b])], i=[(sem, 16)])
            if kmajor:
                dcv = dst[:].rearrange("p (kc o) -> p kc o", o=128)[:, :, h * 32:(h + 1) * 32]
                scv = m["wst"][b][:].rearrange("p (kc o) -> p kc o", o=32)
            else:
                dcv = dst[:, h * 1024:(h + 1) * 1024]
                scv = m["wst"][b][:]
            E("gpsimd", "tensor_copy", out=dcv, in_=scv,
              w=[(sem, CNT[sem]), ("pe", m["wbf_free"][par])], i=[("cvt", 1)])
            m["wst_free"][b] = CNT["cvt"]
        return cur("cvt")

    def proj_matmuls(m, par, pbank, cvt_ready, xn_ready, pb_free_ve, last_stop,
                     rhs=None, i=()):
        rb = xn_sb if rhs is None else rhs
        for kc in range(NCH):
            wv = []
            if kc == 0:
                wv = [cvt_ready, xn_ready, ("ve", pb_free_ve)]
            t_mm(PB[pbank][:, :], m["wbf"][par][:, kc * 128:(kc + 1) * 128],
                 rb[:, kc * TT:(kc + 1) * TT], start=(kc == 0),
                 stop=(last_stop and kc == NCH - 1), w=wv,
                 i=(i if kc == NCH - 1 else ()))
        m["wbf_free"][par] = CNT["pe"]

    arA = Arena("A_")
    wmA = make_wmach(arA, "")
    stgA = [arA.get("stg0", [128, TT], BF16), arA.get("stg1", [128, TT], BF16)]
    sqA = arA.get("sq", [128, TT], BF16)
    stg_free = [0, 0]
    pb_free = [0, 0]

    def v_transpose_store(stg, sq, vview, tt, h, tag_free):
        for q4 in range(4):
            bb = 6 + q4 % 2
            E("tensor", "transpose", out=pb_bf(bb)[:, 0:128],
              in_=stg[:, q4 * 128:(q4 + 1) * 128], identity=c_ident[:],
              w=[cur("ve")], i=[("pe", 1)])
            E("vector", "tensor_copy", out=sq[:, q4 * 128:(q4 + 1) * 128],
              in_=pb_bf(bb)[:, 0:128], w=[cur("pe"), cur("vsq")], i=[("ve", 1)])
        E("sync", "dma_start",
          out=vview[tt * TT:(tt + 1) * TT, h * 128:(h + 1) * 128]
              .rearrange("(qs p) v -> p qs v", p=128),
          in_=sq[:].rearrange("p (qs v) -> p qs v", v=128),
          w=[cur("ve")], i=[("vsq", 16)])

    def attn_a_proj(tt):
        xn_ready = ln_into_xn(tt)
        vview = agin_a[2 * D:3 * D, :].rearrange("(a b) c -> a (b c)", b=4)
        order = list(range(32, 96)) + list(range(0, 32))
        for n, oc in enumerate(order):
            par = n % 2
            cvt_ready = wload(wmA, wq_t, oc, par)
            proj_matmuls(wmA, par, par, cvt_ready, xn_ready, pb_free[par],
                         last_stop=True, i=[("pe", 1)])
            stg = stgA[par]
            ssem = "st%d" % par
            E("vector", "tensor_scalar", out=stg[:], in0=PB[par][:, :],
              scalar1=c_bqkv[:, oc:oc + 1], scalar2=None, op0=ALU.add,
              w=[cur("pe"), (ssem, stg_free[par])], i=[("ve", 1)])
            pb_free[par] = CNT["ve"]
            h = oc % 32
            if 32 <= oc < 64:
                E("sync", "dma_start",
                  out=agin_a[D + h * 128: D + (h + 1) * 128, tt * TT:(tt + 1) * TT],
                  in_=stg[:], w=[cur("ve")], i=[(ssem, 16)])
                stg_free[par] = CNT[ssem]
            elif oc >= 64:
                v_transpose_store(stg, sqA, vview, tt, h, None)
                stg_free[par] = 0
            else:
                E("sync", "dma_start",
                  out=agin_a[h * 128:(h + 1) * 128, tt * TT:(tt + 1) * TT],
                  in_=stg[:], w=[cur("ve")], i=[(ssem, 16)])
                stg_free[par] = CNT[ssem]

    arF = Arena("F_")
    f_q = [arF.get("qb0", [128, TT], BF16), arF.get("qb1", [128, TT], BF16)]
    f_k = [arF.get("kb0", [128, TT], BF16), arF.get("kb1", [128, TT], BF16)]
    f_v = [arF.get("vb0", [128, 4 * 132], BF16), arF.get("vb1", [128, 4 * 132], BF16)]
    f_pt = [arF.get("pt0", [128, 4 * TT], BF16), arF.get("pt1", [128, 4 * TT], BF16)]
    f_tri = arF.get("tri", [128, 4 * TT], BF16)
    f_alc = arF.get("alc", [128, 256], F32)
    f_io = arF.get("io", [1, 128], F32)
    f_sl = arF.get("sl", [1, 256], F32)
    f_ba = arF.get("ba", [1, 256], F32)
    f_nb = [arF.get("nb0", [1, TT], BF16), arF.get("nb1", [1, TT], BF16)]
    f_nbf = arF.get("nbf", [1, TT], BF16)
    f_qsq = arF.get("qsq", [128, TT], BF16)
    f_ksq = arF.get("ksq", [128, TT], BF16)
    f_kn = arF.get("kn", [1, TT], F32)
    f_sr = arF.get("sr", [1, TT], BF16)
    f_scl = arF.get("scl", [1, 16], F32)
    f_colc = arF.get("colc", [1, 1], BF16)
    f_csc = arF.get("csc", [1, 1], BF16)
    f_rc = arF.get("rc", [128, 4], F32)
    f_ao = [arF.get("ao", [128, 512], BF16), arF.get("ao1", [128, 512], BF16)]
    f_ones512 = arF.get("o512", [1, TT], BF16)

    def flash_tables(alibi):
        E("sync", "dma_start", out=f_tri[:], in_=tri, i=[("ext", 16)])
        E("sync", "dma_start", out=f_ones512[:], in_=ones512, i=[("ext", 16)])
        if alibi:
            E("sync", "dma_start", out=f_io[:], in_=iota128f, i=[("ext", 16)])
            E("sync", "dma_start", out=f_sl[:], in_=sloper, i=[("ext", 16)])
            E("sync", "dma_start", out=f_ba[:], in_=baser, i=[("ext", 16)])
        for par in range(2):
            E("vector", "memset", f_v[par][:], 0.0, i=[("ve", 1)])
            for kc in range(4):
                E("vector", "memset", f_v[par][:, kc * 132 + 128:kc * 132 + 129],
                  1.0, i=[("ve", 1)])
        snap = [cur("ext"), cur("ve")]
        for eng in ("tensor", "scalar", "vector", "sync"):
            E(eng, None, w=snap)
        if alibi:
            for piece in range(4):
                bank = piece % 2
                sl_ = slice(piece * 64, (piece + 1) * 64)
                t_mm(PB[bank][:, 0:64], f_io[:], f_sl[:, sl_], start=True,
                     stop=False, w=[cur("ve")], i=())
                t_mm(PB[bank][:, 0:64], c_onesrf[:], f_ba[:, sl_], start=False,
                     stop=True, i=[("pe", 1)])
                E("vector", "tensor_copy", out=f_alc[:, sl_], in_=PB[bank][:, 0:64],
                  w=[cur("pe")], i=[("ve", 1)])

    def posmap(j):
        o = j if j < 8 else 15 - j
        lb = 0 if j < 8 else 1
        return o * 1024 + lb * 512

    def emit_flash(cfg):
        alibi = cfg["alibi"]
        WB = cfg["WB"]
        qloc, kloc, vloc = cfg["qloc"], cfg["kloc"], cfg["vloc"]
        ao_in = cfg["ao_in"]
        kvsh = cfg["kv_shared"]
        flash_tables(alibi)
        expdone = [0, 0, 0, 0]
        qkdone = [0, 0, 0, 0]
        avdone = [0, 0]
        podone = 0
        qfree = [0, 0]
        tile_i = [0]

        def kv_load(s, j, par):
            sem = "kv%d" % par
            krow = 0 if kvsh else s * HD
            jc = posmap(j)
            E("sync", "dma_start", out=f_k[par][:],
              in_=kloc[krow:krow + HD, jc:jc + TT],
              w=[("pe", avdone[par])], i=[(sem, 16)])
            E("sync", "dma_start",
              out=f_v[par][:].rearrange("p (kc v) -> p kc v", v=132)[:, :, 0:128],
              in_=vloc[jc:jc + TT, krow:krow + HD]
                  .rearrange("(kc p) v -> p kc v", p=128),
              i=[(sem, 16)])
            return (sem, CNT[sem])

        for s in range(NSLOT):
            if (not kvsh) or s == 0:
                krow = 0 if kvsh else s * HD
                kn_free = [0, 0]
                knmm = 0
                knve = podone
                for piece in range(NQG):
                    par = piece % 2
                    sem = "kv%d" % par
                    E("sync", "dma_start", out=f_k[par][:],
                      in_=kloc[krow:krow + HD, posmap(piece):posmap(piece) + TT],
                      w=[("pe", avdone[par]), ("se", kn_free[par])],
                      i=[(sem, 16)])
                    E("scalar", "activation", out=f_ksq[:], in_=f_k[par][:],
                      func=AF.Square, w=[(sem, CNT[sem]), ("pe", knmm)],
                      i=[("se", 1)])
                    kn_free[par] = CNT["se"]
                    t_mm(PB[2][:1, 0:TT], c_onesc[:], f_ksq[:], start=True,
                         stop=True, w=[cur("se"), ("ve", knve),
                                       ("se", expdone[2])], i=[("pe", 1)])
                    knmm = CNT["pe"]
                    E("vector", "tensor_copy", out=f_kn[:], in_=PB[2][:1, 0:TT],
                      w=[cur("pe")], i=[("ve", 1)])
                    knve = CNT["ve"]
                    if piece == 0:
                        E("vector", "reduce_sum", out=f_scl[:, 0:1], in_=f_kn[:],
                          axis=AX, i=[("ve", 1)])
                        E("vector", "reduce_max", out=f_scl[:, 1:2], in_=f_kn[:],
                          axis=AX, i=[("ve", 1)])
                    else:
                        E("vector", "reduce_max", out=f_scl[:, 2:3], in_=f_kn[:],
                          axis=AX, i=[("ve", 1)])
                        E("vector", "tensor_max", out=f_scl[:, 1:2],
                          in0=f_scl[:, 1:2], in1=f_scl[:, 2:3], i=[("ve", 1)])
                E("sync", None, w=[cur("se")])
                E("vector", "tensor_mul", out=f_scl[:, 3:4], in0=f_scl[:, 0:1],
                  in1=f_scl[:, 1:2], i=[("ve", 1)])
                E("vector", "tensor_scalar", out=f_scl[:, 3:4], in0=f_scl[:, 3:4],
                  scalar1=1.0 / TT, scalar2=None, op0=ALU.mult, i=[("ve", 1)])
                E("scalar", "activation", out=f_scl[:, 4:5], in_=f_scl[:, 3:4],
                  func=AF.Sqrt, w=[cur("ve")], i=[("se", 1)])
                E("vector", "reciprocal", out=f_scl[:, 5:6], in_=f_scl[:, 0:1],
                  w=[cur("se")], i=[("ve", 1)])
                E("vector", "tensor_mul", out=f_scl[:, 6:7], in0=f_scl[:, 1:2],
                  in1=f_scl[:, 5:6], i=[("ve", 1)])
                E("vector", "tensor_scalar", out=f_scl[:, 6:7], in0=f_scl[:, 6:7],
                  scalar1=float(TT), scalar2=None, op0=ALU.mult, i=[("ve", 1)])
                E("scalar", "activation", out=f_scl[:, 7:8], in_=f_scl[:, 6:7],
                  func=AF.Sqrt, w=[cur("ve")], i=[("se", 1)])
                E("vector", "tensor_scalar", out=f_colc[:], in0=f_scl[:, 7:8],
                  scalar1=-0.5, scalar2=None, op0=ALU.mult, w=[cur("se")],
                  i=[("ve", 1)])
                E("vector", "tensor_scalar", out=f_csc[:], in0=f_scl[:, 4:5],
                  scalar1=-0.5, scalar2=None, op0=ALU.mult, i=[("ve", 1)])

            for qg in range(NQG):
                qpar = qg % 2
                qsem = "qp%d" % qpar
                E("sync", "dma_start", out=f_q[qpar][:],
                  in_=qloc[s * HD:(s + 1) * HD, posmap(qg):posmap(qg) + TT],
                  w=[("pe", qfree[qpar])], i=[(qsem, 16)])
                if alibi:
                    E("sync", "dma_start", out=f_nb[qpar][:],
                      in_=cfg["negaq"][s:s + 1, qg * TT:(qg + 1) * TT],
                      i=[(qsem, 16)])
                qready = (qsem, CNT[qsem])
                E("scalar", "activation", out=f_qsq[:], in_=f_q[qpar][:],
                  func=AF.Square, w=[qready], i=[("se", 1)])
                t_mm(PB[2][:1, 0:TT], c_onesc[:], f_qsq[:], start=True, stop=True,
                     w=[cur("se"), ("se", expdone[2]), ("ve", podone)], i=[("pe", 1)])
                E("vector", "tensor_copy", out=f_sr[:], in_=PB[2][:1, 0:TT],
                  w=[cur("pe")], i=[("ve", 1)])
                t_mm(PB[3][:1, 0:TT], f_colc[:], f_sr[:], start=True, stop=False,
                     w=[cur("ve"), ("se", expdone[3])], i=())
                if alibi:
                    t_mm(PB[3][:1, 0:TT], f_csc[:], f_ones512[:], start=False,
                         stop=False, i=())
                    t_mm(PB[3][:1, 0:TT], c_onesr[:, 0:1], f_nb[qpar][:],
                         start=False, stop=True, w=[qready], i=[("pe", 1)])
                else:
                    t_mm(PB[3][:1, 0:TT], f_csc[:], f_ones512[:], start=False,
                         stop=True, i=[("pe", 1)])
                E("vector", "tensor_copy", out=f_nbf[:], in_=PB[3][:1, 0:TT],
                  w=[cur("pe")], i=[("ve", 1)])
                nbready = cur("ve")

                jlo = max(0, qg - WB[s])
                js = list(range(jlo, qg + 1))
                kvready = {}
                kvready[js[0]] = kv_load(s, js[0], tile_i[0] % 2)
                first = True
                for jn, j in enumerate(js):
                    par = tile_i[0] % 2
                    if jn + 1 < len(js):
                        kvready[js[jn + 1]] = kv_load(s, js[jn + 1], (tile_i[0] + 1) % 2)
                    for kc in range(4):
                        sb = kc
                        t_mm(PB[sb][:, 0:TT], c_onesr[:], f_nbf[:],
                             start=True, stop=False,
                             w=[kvready[j], qready, ("se", expdone[sb]),
                                ("ve", nbready[1])], i=())
                        t_mm(PB[sb][:, 0:TT], f_k[par][:, kc * 128:(kc + 1) * 128],
                             f_q[qpar][:], start=False, stop=True, i=[("pe", 1)])
                        qkdone[sb] = CNT["pe"]
                        expw = [("pe", qkdone[sb]), ("pe", avdone[par])]
                        if j == qg:
                            E("vector", "tensor_tensor", out=PB[sb][:, 0:TT],
                              in0=PB[sb][:, 0:TT],
                              in1=f_tri[:, kc * TT:(kc + 1) * TT], op=ALU.min,
                              w=[("pe", qkdone[sb])], i=[("ve", 1)])
                            expw.append(("ve", CNT["ve"]))
                        if alibi:
                            bias = f_alc[:, s * 64 + j * 4 + kc: s * 64 + j * 4 + kc + 1]
                            E("scalar", "activation",
                              out=f_pt[par][:, kc * TT:(kc + 1) * TT],
                              in_=PB[sb][:, 0:TT], func=AF.Exp, bias=bias,
                              scale=SCALE, w=expw, i=[("se", 1)])
                        else:
                            E("scalar", "activation",
                              out=f_pt[par][:, kc * TT:(kc + 1) * TT],
                              in_=PB[sb][:, 0:TT], func=AF.Exp, scale=SCALE,
                              w=expw, i=[("se", 1)])
                        expdone[sb] = CNT["se"]
                    for kc in range(4):
                        for qs in range(4):
                            wv = []
                            if kc == 0 and qs == 0:
                                wv = [cur("se")]
                                if first:
                                    wv.append(("ve", podone))
                                wv.append(kvready[j])
                            last_av_kb = (kc == 3 and qs == 3)
                            t_mm(PB[4 + qs][:, 0:129],
                                 f_pt[par][:, kc * TT + qs * 128: kc * TT + (qs + 1) * 128],
                                 f_v[par][:, kc * 132:kc * 132 + 129],
                                 start=(first and kc == 0), stop=(j == js[-1] and kc == 3),
                                 w=wv, i=([("pe", 1)] if last_av_kb else ()))
                    avdone[par] = CNT["pe"]
                    first = False
                    tile_i[0] += 1
                qfree[qpar] = CNT["pe"]
                aop = qg % 2
                aosem = "ao%d" % aop
                for qs in range(4):
                    E("vector", "reciprocal", out=f_rc[:, qs:qs + 1],
                      in_=PB[4 + qs][:, 128:129],
                      w=[cur("pe")] if qs == 0 else [], i=[("ve", 1)])
                    E("vector", "tensor_scalar",
                      out=f_ao[aop][:, qs * 128:(qs + 1) * 128],
                      in0=PB[4 + qs][:, 0:128], scalar1=f_rc[:, qs:qs + 1],
                      scalar2=None, op0=ALU.mult,
                      w=[(aosem, CNT[aosem])] if qs == 0 else [], i=[("ve", 1)])
                podone = CNT["ve"]
                E("sync", "dma_start",
                  out=ao_in[qg * TT:(qg + 1) * TT, s * HD:(s + 1) * HD]
                      .rearrange("(qs p) h -> p qs h", p=128),
                  in_=f_ao[aop][:].rearrange("p (qs h) -> p qs h", h=128),
                  w=[cur("ve")], i=[(aosem, 16)])

    def extract_a():
        BW = 3 * D

        def run_sw(h):
            for arm in h.switch_core_id(n=128):
                c = arm.logical % 8
                for part, dst in ((0, q_loc), (1, k_loc)):
                    for cb in range(2):
                        for sl in range(NSLOT):
                            src = bass.AP(
                                agout_a.tensor,
                                (BW * 0 + part * D + (8 * sl + c) * 128) * TPC + cb * 512,
                                [[BW * TPC, NCORES], [TPC, 128], [1, TT]])
                            dstap = bass.AP(
                                dst.tensor,
                                sl * 128 * S + cb * 512,
                                [[1024, NCORES], [S, 128], [1, TT]])
                            h.dma_start(out=dstap, in_=src).then_inc(SEM["ext"], 16)
                for cb in range(2):
                    for sl in range(NSLOT):
                        src = bass.AP(
                            agout_a.tensor,
                            2 * D * TPC + cb * 512 * D + (8 * sl + c) * 128,
                            [[BW * TPC, NCORES], [D, TT], [1, 128]])
                        dstap = bass.AP(
                            v_loc.tensor,
                            cb * 512 * 512 + sl * 128,
                            [[TPC * 512, NCORES], [512, TT], [1, 128]])
                        h.dma_start(out=dstap, in_=src).then_inc(SEM["ext"], 16)
        PLAN["sync"].append(run_sw)
        CNT["ext"] += 16 * 24
        E("sync", None, w=[cur("ext")])
        for eng in ("tensor", "scalar", "vector"):
            E(eng, None, w=[cur("ext")])

    def extract_b():
        BWB = D + 2 * KVH * HD

        def run_sw(h):
            for arm in h.switch_core_id(n=128):
                c = arm.logical % 8
                for cb in range(2):
                    for sl in range(NSLOT):
                        src = bass.AP(
                            agout_b.tensor,
                            ((4 * c + sl) * 128) * TPC + cb * 512,
                            [[BWB * TPC, NCORES], [TPC, 128], [1, TT]])
                        dstap = bass.AP(
                            q_locb.tensor,
                            sl * 128 * S + cb * 512,
                            [[1024, NCORES], [S, 128], [1, TT]])
                        h.dma_start(out=dstap, in_=src).then_inc(SEM["ext"], 16)
                    src = bass.AP(
                        agout_b.tensor,
                        (D + 128 * c) * TPC + cb * 512,
                        [[BWB * TPC, NCORES], [TPC, 128], [1, TT]])
                    dstap = bass.AP(
                        k_locb.tensor, cb * 512,
                        [[1024, NCORES], [S, 128], [1, TT]])
                    h.dma_start(out=dstap, in_=src).then_inc(SEM["ext"], 16)
                    src = bass.AP(
                        agout_b.tensor,
                        (D + KVH * HD) * TPC + cb * 512 * KVH * HD + 128 * c,
                        [[BWB * TPC, NCORES], [KVH * HD, TT], [1, 128]])
                    dstap = bass.AP(
                        v_locb.tensor, cb * 512 * HD,
                        [[TPC * HD, NCORES], [HD, TT], [1, 128]])
                    h.dma_start(out=dstap, in_=src).then_inc(SEM["ext"], 16)
        PLAN["sync"].append(run_sw)
        CNT["ext"] += 16 * 12
        E("sync", None, w=[cur("ext")])
        for eng in ("tensor", "scalar", "vector"):
            E(eng, None, w=[cur("ext")])

    def extract_ao(ao_out, ao_loc):
        def run_sw(h):
            for arm in h.switch_core_id(n=128):
                c = arm.logical % 8
                for cb in range(2):
                    tabs = (c if cb == 0 else 15 - c) * 512
                    for sl in range(NSLOT):
                        src = bass.AP(
                            ao_out.tensor,
                            tabs * 512 + sl * 128,
                            [[S * 512, NCORES], [512, TT], [1, 128]])
                        dstap = bass.AP(
                            ao_loc.tensor,
                            cb * 512 * D + sl * 128,
                            [[NSLOT * 128, NCORES], [D, TT], [1, 128]])
                        h.dma_start(out=dstap, in_=src).then_inc(SEM["ext"], 16)
        PLAN["sync"].append(run_sw)
        CNT["ext"] += 16 * 8
        E("sync", None, w=[cur("ext")])
        for eng in ("tensor", "scalar", "vector"):
            E(eng, None, w=[cur("ext")])

    def o_proj(ao_loc, w_src=None, bias=None, dequant=None, wm=None, stg2=None):
        stg_free = [CNT["pe"], CNT["pe"]]
        for tt in range(NTT):
            for cidx in range(32):
                for q4 in range(4):
                    stg = stg2[q4 % 2]
                    sem = "kv%d" % (q4 % 2)
                    E("sync", "dma_start", w=[("pe", stg_free[q4 % 2])],
                      in_=ao_loc[tt * TT + q4 * 128: tt * TT + (q4 + 1) * 128,
                                 cidx * 128:(cidx + 1) * 128],
                      out=stg[:, 0:128], i=[(sem, 16)])
                    bb = 6 + q4 % 2
                    E("tensor", "transpose", out=pb_bf(bb)[:, 0:128],
                      in_=stg[:, 0:128], identity=c_ident[:],
                      w=[(sem, CNT[sem]), cur("ve")], i=[("pe", 1)])
                    stg_free[q4 % 2] = CNT["pe"]
                    E("vector", "tensor_copy",
                      out=xn_sb[:, cidx * TT + q4 * 128: cidx * TT + (q4 + 1) * 128],
                      in_=pb_bf(bb)[:, 0:128], w=[cur("pe")], i=[("ve", 1)])
            E("vector", "engine_nop", i=[("ve", 1)])
            xn_ready = cur("ve")
            opb_free = [0, 0]
            for oc in range(32):
                par = oc % 2
                if dequant is None:
                    cvt_ready = wload(wm, w_src, oc, par)
                else:
                    cvt_ready = dequant(oc, par)
                proj_matmuls(wm, par, par, cvt_ready, xn_ready, opb_free[par],
                             last_stop=True, i=[("pe", 1)])
                xc = x_sb[:, oc * TPC + tt * TT: oc * TPC + tt * TT + TT]
                if bias is not None:
                    E("vector", "scalar_tensor_tensor", out=xc, in0=PB[par][:, :],
                      scalar=bias[:, oc:oc + 1], in1=xc, op0=ALU.add, op1=ALU.add,
                      w=[cur("pe")], i=[("ve", 1)])
                else:
                    E("vector", "tensor_add", out=xc, in0=PB[par][:, :], in1=xc,
                      w=[cur("pe")], i=[("ve", 1)])
                opb_free[par] = CNT["ve"]

    def phase_attn_a():
        for tt in range(NTT):
            attn_a_proj(tt)
        E("gpsimd", "collective_compute", "AllGather", ALU.bypass,
          [list(range(NCORES))], [agin_a.opt()], [agout_a.opt()],
          w=[cur("st0"), cur("st1"), cur("vsq")], i=[("cc", 1)])
        barrier()
        if SUBSTAGE < 2:
            return
        extract_a()
        if SUBSTAGE < 3:
            return
        emit_flash(dict(alibi=True, WB=WB_A, qloc=q_loc, kloc=k_loc,
                        vloc=v_loc, ao_in=aoin_a, negaq=negaq, kv_shared=False))
        if SUBSTAGE < 4:
            return
        E("gpsimd", "collective_compute", "AllGather", ALU.bypass,
          [list(range(NCORES))], [aoin_a.opt()], [aoout_a.opt()],
          w=[cur("ao0"), cur("ao1")], i=[("cc", 1)])
        barrier()
        extract_ao(aoout_a, ao_loc_a)
        arO = Arena("Oa_")
        wmO = make_wmach(arO, "")
        stgO = [arO.get("st0", [128, TT], BF16), arO.get("st1", [128, TT], BF16)]
        o_proj(ao_loc_a, w_src=wo_t, bias=c_bout, wm=wmO, stg2=stgO)
        barrier()

    def phase_ffn(idx):
        arN = Arena("N%d_" % idx)
        wmN = make_wmach(arN, "")
        b2bf = arN.get("b2", [128, 4096], BF16)
        g_lr = arN.get("lr", [8, TT], BF16)
        g_lt = arN.get("lt", [128, 32], F32)
        g_wt = arN.get("wt", [128, 32], BF16)
        g_m = arN.get("m", [128, 4], F32)
        g_msk = arN.get("msk", [128, 8], F32)
        g_l2 = arN.get("l2", [128, 8], F32)
        g_kp = arN.get("kp", [128, 8], F32)
        g_u = arN.get("u", [128, 8], F32)
        g_wf = arN.get("wf", [8, TT], BF16)
        g_we = arN.get("we", [128, TT], BF16)
        g_g = arN.get("g", [128, TT], BF16)
        rbf = arN.get("rbf", [128, 256], BF16)

        wload(wmN, wb2, 0, 0, kmajor=False, dst=b2bf)
        E("scalar", "dma_start", out=wmN["wst"][0][:, 0:256]
          .rearrange("p (kc o) -> p kc o", o=8),
          in_=wr_t.rearrange("(kc p) o -> p kc o", p=128),
          w=[("cvt", wmN["wst_free"][0])], i=[("w0", 16)])
        E("gpsimd", "tensor_copy", out=rbf[:], in_=wmN["wst"][0][:, 0:256],
          w=[cur("w0")], i=[("cvt", 1)])
        wmN["wst_free"][0] = CNT["cvt"]
        abreQ = cur("cvt")

        for tt in range(NTT):
            xn_ready = ln_into_xn(tt)
            for kc in range(NCH):
                t_mm(PB[2][0:8, 0:TT], rbf[:, kc * 8:(kc + 1) * 8],
                     xn_sb[:, kc * TT:(kc + 1) * TT], start=(kc == 0),
                     stop=(kc == NCH - 1),
                     w=([abreQ, xn_ready] if kc == 0 else []),
                     i=([("pe", 1)] if kc == NCH - 1 else ()))
            E("vector", "tensor_copy", out=g_lr[:], in_=PB[2][0:8, 0:TT],
              w=[cur("pe")], i=[("ve", 1)])
            for q4 in range(4):
                E("tensor", "transpose", out=pb_bf(3)[:, 0:8],
                  in_=g_lr[:, q4 * 128:(q4 + 1) * 128], identity=c_ident[0:8, 0:8],
                  w=[cur("ve")], i=[("pe", 1)])
                E("vector", "tensor_copy", out=g_lt[:, q4 * 8:(q4 + 1) * 8],
                  in_=pb_bf(3)[:, 0:8], w=[cur("pe")], i=[("ve", 1)])
            for q4 in range(4):
                lt = g_lt[:, q4 * 8:(q4 + 1) * 8]
                E("vector", "reduce_max", out=g_m[:, 0:1], in_=lt, axis=AX,
                  negate=True, i=[("ve", 1)])
                E("vector", "tensor_scalar", out=g_msk[:], in0=lt,
                  scalar1=g_m[:, 0:1], scalar2=None, op0=ALU.add, i=[("ve", 1)])
                E("vector", "tensor_scalar", out=g_l2[:], in0=g_msk[:],
                  scalar1=0.0, scalar2=None, op0=ALU.is_ge, i=[("ve", 1)])
                E("vector", "scalar_tensor_tensor", out=g_l2[:], in0=g_l2[:],
                  scalar=-1e30, in1=g_msk[:], op0=ALU.mult, op1=ALU.add,
                  i=[("ve", 1)])
                E("vector", "reduce_max", out=g_m[:, 1:2], in_=g_l2[:], axis=AX,
                  negate=True, i=[("ve", 1)])
                E("vector", "tensor_scalar", out=g_kp[:], in0=g_msk[:],
                  scalar1=g_m[:, 1:2], scalar2=None, op0=ALU.add, i=[("ve", 1)])
                E("vector", "tensor_scalar", out=g_kp[:], in0=g_kp[:],
                  scalar1=0.0, scalar2=None, op0=ALU.is_ge, i=[("ve", 1)])
                E("scalar", "activation", out=g_u[:], in_=g_msk[:], func=AF.Exp,
                  w=[cur("ve")], i=[("se", 1)])
                E("vector", "tensor_mul", out=g_u[:], in0=g_u[:], in1=g_kp[:],
                  w=[cur("se")], i=[("ve", 1)])
                E("vector", "reduce_sum", out=g_m[:, 2:3], in_=g_u[:], axis=AX,
                  i=[("ve", 1)])
                E("vector", "reciprocal", out=g_m[:, 2:3], in_=g_m[:, 2:3],
                  i=[("ve", 1)])
                E("vector", "tensor_scalar", out=g_wt[:, q4 * 8:(q4 + 1) * 8],
                  in0=g_u[:], scalar1=g_m[:, 2:3], scalar2=2.0, op0=ALU.mult,
                  op1=ALU.mult, i=[("ve", 1)])
            for q4 in range(4):
                E("tensor", "transpose", out=pb_bf(3)[0:8, 0:128],
                  in_=g_wt[:, q4 * 8:(q4 + 1) * 8], identity=c_ident[:],
                  w=[cur("ve")], i=[("pe", 1)])
                E("vector", "tensor_copy", out=g_wf[:, q4 * 128:(q4 + 1) * 128],
                  in_=pb_bf(3)[0:8, 0:128], w=[cur("pe")], i=[("ve", 1)])
            t_mm(PB[3][:, 0:TT], c_expand[:], g_wf[:], start=True, stop=True,
                 w=[cur("ve")], i=[("pe", 1)])
            E("vector", "tensor_copy", out=g_we[:], in_=PB[3][:, 0:TT],
              w=[cur("pe")], i=[("ve", 1)])
            acvt = wload(wmN, wa_t, 0, 1)
            for kc in range(NCH):
                t_mm(PB[2][:, 0:TT], wmN["wbf"][1][:, kc * 128:(kc + 1) * 128],
                     xn_sb[:, kc * TT:(kc + 1) * TT], start=(kc == 0),
                     stop=(kc == NCH - 1),
                     w=([cur("ve"), acvt] if kc == 0 else []),
                     i=([("pe", 1)] if kc == NCH - 1 else ()))
            wmN["wbf_free"][1] = CNT["pe"]
            E("vector", "tensor_mul", out=g_g[:], in0=PB[2][:, 0:TT], in1=g_we[:],
              w=[cur("pe")], i=[("ve", 1)])
            gready = cur("ve")

            fpb_free = [0, 0]
            for oc in range(32):
                par = oc % 2
                cvt_ready = wload(wmN, wb_t, oc, par)
                proj_matmuls(wmN, par, par, cvt_ready, xn_ready, fpb_free[par],
                             last_stop=False)
                t_mm(PB[par][:, :], b2bf[:, oc * 128:(oc + 1) * 128], g_g[:],
                     start=False, stop=True, w=([gready] if oc < 2 else []),
                     i=[("pe", 1)])
                wmN["wbf_free"][par] = CNT["pe"]
                xc = x_sb[:, oc * TPC + tt * TT: oc * TPC + tt * TT + TT]
                E("vector", "scalar_tensor_tensor", out=xc, in0=PB[par][:, :],
                  scalar=c_bbase[:, oc:oc + 1], in1=xc, op0=ALU.add, op1=ALU.add,
                  w=[cur("pe")], i=[("ve", 1)])
                fpb_free[par] = CNT["ve"]
        barrier()

    def phase_attn_b():
        arB = Arena("B_")
        wmB = make_wmach(arB, "")
        cstI = [wmB["wst"][0].ap().bitcast(I32), wmB["wst"][1].ap().bitcast(I32)]
        srowf = [arB.get("sf0", [1, TT], F32)]
        srow = [arB.get("sr0", [1, TT], BF16), arB.get("sr1", [1, TT], BF16)]
        sbb = [arB.get("sb0", [128, TT], BF16), arB.get("sb1", [128, TT], BF16)]
        ropA = arB.get("rA", [128, TT], BF16)
        ropB = arB.get("rB", [128, TT], BF16)
        rt1 = arB.get("rt1", [128, TT], BF16)
        rt2 = arB.get("rt2", [128, TT], BF16)
        stgB = [arB.get("st0", [128, TT], BF16), arB.get("st1", [128, TT], BF16)]
        sqB = rt1

        srow_free = [0, 0]

        def dequant(wm, wcodes, wscales, oc, par):
            for h in range(4):
                b = wm["hidx"] % 2
                wm["hidx"] += 1
                sem = "w%d" % b
                E("scalar", "dma_start",
                  out=cstI[b][:],
                  in_=wcodes[:, (oc * 4 + h) * 1024:(oc * 4 + h + 1) * 1024],
                  w=[("se", wm["wst_free"][b])], i=[(sem, 16)])
                E("scalar", "activation",
                  out=wm["wbf"][par][:].rearrange("p (kc o) -> p kc o", o=128)[:, :, h * 32:(h + 1) * 32],
                  in_=cstI[b][:].rearrange("p (kc o) -> p kc o", o=32),
                  func=AF.Identity, bias=c_neg4[:], scale=0.5,
                  w=[(sem, CNT[sem]), ("pe", wm["wbf_free"][par]),
                     ("ve", wm.setdefault("sc_done", [0, 0])[par])],
                  i=[("se", 1)])
                wm["wst_free"][b] = CNT["se"]
            for pc in range(8):
                sp = pc % 2
                E("scalar", "dma_start",
                  out=srowf[0][:].rearrange("a (g o) -> a g o", o=128),
                  in_=wscales[pc * 4:(pc + 1) * 4, oc * 128:(oc + 1) * 128]
                      .unsqueeze(0),
                  w=[("ve", srow_free[0])], i=[("sr", 16)])
                E("vector", "tensor_copy", out=srow[sp][:], in_=srowf[0][:],
                  w=[("sr", CNT["sr"]),
                     ("pe", wm.setdefault("srd", [0, 0])[sp])], i=[("ve", 1)])
                srow_free[0] = CNT["ve"]
                bank = 2 + sp
                t_mm(PB[bank][:, 0:TT], c_onesr[:], srow[sp][:], start=True,
                     stop=True, w=[cur("ve"),
                                   ("ve", wm.setdefault("sb_done", [0, 0])[sp])],
                     i=[("pe", 1)])
                wm["srd"][sp] = CNT["pe"]
                E("vector", "tensor_copy", out=sbb[sp][:], in_=PB[bank][:, 0:TT],
                  w=[cur("pe")], i=[("ve", 1)])
                E("vector", "tensor_mul",
                  out=wm["wbf"][par][:, pc * TT:(pc + 1) * TT],
                  in0=wm["wbf"][par][:, pc * TT:(pc + 1) * TT], in1=sbb[sp][:],
                  w=[cur("se")], i=[("ve", 1)])
                wm["sb_done"][sp] = CNT["ve"]
            wm["sc_done"][par] = CNT["ve"]
            return cur("ve")

        def rope_tile(stg):
            E("vector", "tensor_mul", out=rt1[0:64, :], in0=stg[0:64, :],
              in1=ropA[0:64, :], w=[cur("vsq")], i=[("ve", 1)])
            E("vector", "tensor_mul", out=rt2[0:64, :], in0=stg[64:128, :],
              in1=ropA[64:128, :], i=[("ve", 1)])
            E("vector", "tensor_mul", out=rt1[64:128, :], in0=stg[64:128, :],
              in1=ropB[64:128, :], i=[("ve", 1)])
            E("vector", "tensor_mul", out=rt2[64:128, :], in0=stg[0:64, :],
              in1=ropB[0:64, :], i=[("ve", 1)])
            E("vector", "tensor_sub", out=stg[0:64, :], in0=rt1[0:64, :],
              in1=rt2[0:64, :], i=[("ve", 1)])
            E("vector", "tensor_add", out=stg[64:128, :], in0=rt1[64:128, :],
              in1=rt2[64:128, :], i=[("ve", 1)])

        BWB = D + 2 * KVH * HD
        bstg_free = [0, 0]
        bpb_free = [0, 0]
        for tt in range(NTT):
            E("sync", "dma_start", out=ropA[:], in_=ropeA[:, tt * TT:(tt + 1) * TT],
              w=[cur("ve")], i=[("ext", 16)])
            E("sync", "dma_start", out=ropB[:], in_=ropeB[:, tt * TT:(tt + 1) * TT],
              i=[("ext", 16)])
            E("vector", None, w=[cur("ext")])
            xn_ready = ln_into_xn(tt)
            vview = agin_b[D + KVH * HD: D + 2 * KVH * HD, :]
            specs = [("k", kcod, kscl, KVH), ("v", vcod, vscl, KVH),
                     ("q", qcod, qscl, 32)]
            n = 0
            for kind, wc, ws, cnt_ in specs:
                for h in range(cnt_):
                    par = n % 2
                    cvt_ready = dequant(wmB, wc, ws, h, par)
                    proj_matmuls(wmB, par, par, cvt_ready, xn_ready,
                                 bpb_free[par], last_stop=True, i=[("pe", 1)])
                    stg = stgB[par]
                    ssem = "st%d" % par
                    E("vector", "tensor_copy", out=stg[:], in_=PB[par][:, :],
                      w=[cur("pe"), (ssem, bstg_free[par])], i=[("ve", 1)])
                    bpb_free[par] = CNT["ve"]
                    if kind == "k":
                        rope_tile(stg)
                        E("sync", "dma_start",
                          out=agin_b[D + h * 128: D + (h + 1) * 128,
                                     tt * TT:(tt + 1) * TT],
                          in_=stg[:], w=[cur("ve")], i=[(ssem, 16)])
                        bstg_free[par] = CNT[ssem]
                    elif kind == "v":
                        v_transpose_store(stg, sqB, vview, tt, h, None)
                        bstg_free[par] = 0
                    else:
                        rope_tile(stg)
                        E("sync", "dma_start",
                          out=agin_b[h * 128:(h + 1) * 128, tt * TT:(tt + 1) * TT],
                          in_=stg[:], w=[cur("ve")], i=[(ssem, 16)])
                        bstg_free[par] = CNT[ssem]
                    n += 1
        E("gpsimd", "collective_compute", "AllGather", ALU.bypass,
          [list(range(NCORES))], [agin_b.opt()], [agout_b.opt()],
          w=[cur("st0"), cur("st1"), cur("vsq")], i=[("cc", 1)])
        barrier()
        extract_b()
        emit_flash(dict(alibi=False, WB=WB_B, qloc=q_locb, kloc=k_locb,
                        vloc=v_locb, ao_in=aoin_b, negaq=None, kv_shared=True))
        E("gpsimd", "collective_compute", "AllGather", ALU.bypass,
          [list(range(NCORES))], [aoin_b.opt()], [aoout_b.opt()],
          w=[cur("ao0"), cur("ao1")], i=[("cc", 1)])
        barrier()
        extract_ao(aoout_b, ao_loc_b)
        arO = Arena("Ob_")
        wmO = make_wmach(arO, "")
        cstO = [wmO["wst"][0].ap().bitcast(I32), wmO["wst"][1].ap().bitcast(I32)]
        srowOf = [arO.get("sf0", [1, TT], F32), arO.get("sf1", [1, TT], F32)]
        srowO = [arO.get("sr0", [1, TT], BF16), arO.get("sr1", [1, TT], BF16)]
        sbO = [arO.get("sb0", [128, TT], BF16), arO.get("sb1", [128, TT], BF16)]
        stgO = [arO.get("st0", [128, TT], BF16), arO.get("st1", [128, TT], BF16)]
        srowO_free = [0, 0]

        def dequantO(oc, par, wm=wmO):
            for h in range(4):
                b = wm["hidx"] % 2
                wm["hidx"] += 1
                sem = "w%d" % b
                E("scalar", "dma_start",
                  out=cstO[b][:],
                  in_=ocod[:, (oc * 4 + h) * 1024:(oc * 4 + h + 1) * 1024],
                  w=[("se", wm["wst_free"][b])], i=[(sem, 16)])
                E("scalar", "activation",
                  out=wm["wbf"][par][:].rearrange("p (kc o) -> p kc o", o=128)[:, :, h * 32:(h + 1) * 32],
                  in_=cstO[b][:].rearrange("p (kc o) -> p kc o", o=32),
                  func=AF.Identity, bias=c_neg4[:], scale=0.5,
                  w=[(sem, CNT[sem]), ("pe", wm["wbf_free"][par]),
                     ("ve", wm.setdefault("sc_done", [0, 0])[par])],
                  i=[("se", 1)])
                wm["wst_free"][b] = CNT["se"]
            for pc in range(8):
                sp = pc % 2
                E("scalar", "dma_start",
                  out=srowOf[0][:].rearrange("a (g o) -> a g o", o=128),
                  in_=oscl[pc * 4:(pc + 1) * 4, oc * 128:(oc + 1) * 128]
                      .unsqueeze(0),
                  w=[("ve", srowO_free[0])], i=[("sr", 16)])
                E("vector", "tensor_copy", out=srowO[sp][:], in_=srowOf[0][:],
                  w=[("sr", CNT["sr"]),
                     ("pe", wm.setdefault("srd", [0, 0])[sp])], i=[("ve", 1)])
                srowO_free[0] = CNT["ve"]
                bank = 2 + sp
                t_mm(PB[bank][:, 0:TT], c_onesr[:], srowO[sp][:], start=True,
                     stop=True, w=[cur("ve"),
                                   ("ve", wm.setdefault("sb_done", [0, 0])[sp])],
                     i=[("pe", 1)])
                wm["srd"][sp] = CNT["pe"]
                E("vector", "tensor_copy", out=sbO[sp][:], in_=PB[bank][:, 0:TT],
                  w=[cur("pe")], i=[("ve", 1)])
                E("vector", "tensor_mul",
                  out=wm["wbf"][par][:, pc * TT:(pc + 1) * TT],
                  in0=wm["wbf"][par][:, pc * TT:(pc + 1) * TT], in1=sbO[sp][:],
                  w=[cur("se")], i=[("ve", 1)])
                wm["sb_done"][sp] = CNT["ve"]
            wm["sc_done"][par] = CNT["ve"]
            return cur("ve")

        o_proj(ao_loc_b, w_src=None, bias=None, dequant=dequantO, wm=wmO,
               stg2=stgO)
        barrier()

    if PHASES >= 1:
        phase_attn_a()
    if PHASES >= 2:
        phase_ffn(0)
    if PHASES >= 3:
        phase_attn_b()
    if PHASES >= 4:
        phase_ffn(1)

    E("sync", "dma_start", out=xout.rearrange("(c p) t -> p c t", p=128),
      in_=x_sb[:].rearrange("p (c t) -> p c t", t=TPC),
      w=[cur("ve"), cur("pe"), cur("se")], i=[("fin", 16)])
    E("sync", None, w=[("fin", 16)])
    E("gpsimd", None, w=[("fin", 16)])

    with nc.Block() as block:
        for eng in ENGS:
            getattr(block, eng)(lambda h, _e=eng: [f(h) for f in PLAN[_e]])
    stack.close()
    return nc



def _alibi_slopes():
    m = (np.arange(1, 33, dtype=np.float64)) * 0.25
    return 1.0 / (2.0 ** m)


def _ilv(wt):
    Dk, O = wt.shape
    a = wt.reshape(Dk // 128, 128, O // 32, 32)
    return np.ascontiguousarray(a.transpose(1, 2, 0, 3).reshape(128, -1))


def _host_stage(inputs):
    f32 = np.float32
    x = np.asarray(inputs["L0_attn_a_x"], dtype=f32).reshape(S, D)
    slopes = _alibi_slopes()

    stg = {}
    stg["wq_t"] = _ilv(np.asarray(inputs["L0_attn_a_w_qkv"], f32).T)
    stg["bqkv"] = np.ascontiguousarray(
        np.asarray(inputs["L0_attn_a_b_qkv"], f32).reshape(96, 128).T)
    perm = np.empty(D, dtype=np.int64)
    for r in range(8):
        for s in range(4):
            h = 8 * s + r
            perm[(r * 4 + s) * 128:(r * 4 + s + 1) * 128] = np.arange(h * 128, (h + 1) * 128)
    stg["wo_t"] = _ilv(np.asarray(inputs["L0_attn_a_w_out"], f32).T[perm, :])
    stg["bout"] = np.ascontiguousarray(
        np.asarray(inputs["L0_attn_a_b_out"], f32).reshape(32, 128).T)
    stg["wb_t"] = _ilv(np.asarray(inputs["L0_ffn_W_base"], f32).T)
    stg["bbase"] = np.ascontiguousarray(
        np.asarray(inputs["L0_ffn_b_base"], f32).reshape(32, 128).T)
    stg["wr_t"] = np.ascontiguousarray(np.asarray(inputs["L0_ffn_W_router"], f32).T)
    stg["wa_t"] = _ilv(np.asarray(inputs["L0_ffn_lora_A"], f32).reshape(128, D).T)
    stg["wb2"] = np.ascontiguousarray(
        np.asarray(inputs["L0_ffn_lora_B"], f32).transpose(0, 2, 1).reshape(128, D))
    stg["qcod"] = _ilv(np.asarray(inputs["L0_attn_b_q_codes"], np.int32))
    stg["kcod"] = _ilv(np.asarray(inputs["L0_attn_b_k_codes"], np.int32))
    stg["vcod"] = _ilv(np.asarray(inputs["L0_attn_b_v_codes"], np.int32))
    stg["qscl"] = np.ascontiguousarray(np.asarray(inputs["L0_attn_b_q_scales"], f32))
    stg["kscl"] = np.ascontiguousarray(np.asarray(inputs["L0_attn_b_k_scales"], f32))
    stg["vscl"] = np.ascontiguousarray(np.asarray(inputs["L0_attn_b_v_scales"], f32))
    permb = np.empty(D, dtype=np.int64)
    for r in range(8):
        for s in range(4):
            h = 4 * r + s
            permb[(r * 4 + s) * 128:(r * 4 + s + 1) * 128] = np.arange(h * 128, (h + 1) * 128)
    stg["ocod"] = _ilv(np.asarray(inputs["L0_attn_b_o_codes"], np.int32)[permb, :])
    gperm = permb[::128] // 128
    stg["oscl"] = np.ascontiguousarray(
        np.asarray(inputs["L0_attn_b_o_scales"], f32)[gperm, :])

    stg["ident"] = np.eye(128, dtype=BF)
    stg["identf"] = np.eye(128, dtype=f32)
    stg["onesr"] = np.ones((1, 128), BF)
    stg["onesrf"] = np.ones((1, 128), f32)
    stg["onesc"] = np.ones((128, 1), BF)
    stg["ones512"] = np.ones((1, TT), BF)
    tri = np.zeros((128, 4 * TT), f32)
    for kc in range(4):
        for kp in range(128):
            tri[kp, kc * TT:(kc + 1) * TT] = np.where(
                np.arange(TT) >= (kc * 128 + kp), 3e38, -1e9)
    stg["tri"] = tri.astype(BF)
    stg["epsap"] = np.full((1, 1), EPS, f32)
    stg["iota128f"] = np.arange(128, dtype=f32).reshape(1, 128)
    exp_ = np.zeros((8, 128), BF)
    for e in range(8):
        exp_[e, e * 16:(e + 1) * 16] = 1.0
    stg["expand"] = exp_

    per_core = []
    for c in range(NCORES):
        rows = np.concatenate([np.arange(c * 512, (c + 1) * 512),
                               np.arange((15 - c) * 512, (16 - c) * 512)])
        d = dict(stg)
        d["x0"] = np.ascontiguousarray(x[rows, :].T)
        pos = rows.astype(np.float64)
        inv = 1.0 / (10000.0 ** (np.arange(64, dtype=np.float64) / 64.0))
        ang = pos[None, :] * inv[:, None]
        ra = np.zeros((128, TPC), np.float64)
        rb = np.zeros((128, TPC), np.float64)
        ra[0:64] = np.cos(ang)
        ra[64:128] = np.sin(ang)
        rb[0:64] = np.sin(ang)
        rb[64:128] = np.cos(ang)
        d["ropeA"] = ra.astype(BF)
        d["ropeB"] = rb.astype(BF)
        sl = np.zeros((1, 256), f32)
        ba = np.zeros((1, 256), f32)
        nq = np.zeros((4, S), np.float64)
        for s in range(4):
            h = 8 * s + c
            sp = slopes[h]
            sl[0, s * 64:(s + 1) * 64] = sp
            ba[0, s * 64:(s + 1) * 64] = sp * (128.0 * np.arange(64) - (S - 1))
            nq[s, :] = -SQHD * sp * (np.arange(S, dtype=np.float64) - (S - 1))
        d["sloper"] = sl
        d["baser"] = ba
        d["negaq"] = nq.astype(BF)
        per_core.append(d)
    return per_core


def kernel(**inputs):
    per_core = _host_stage(inputs)
    nc = build_program()
    res = run_bass_kernel_spmd(nc, per_core, core_ids=list(range(NCORES)))
    out = np.zeros((S, D), dtype=np.float32)
    for c in range(NCORES):
        r = np.asarray(res.results[c]["xout"])
        rows = np.concatenate([np.arange(c * 512, (c + 1) * 512),
                               np.arange((15 - c) * 512, (16 - c) * 512)])
        out[rows, :] = r.T
    return out.reshape(1, S, D)


# version 03 (final)
# speedup vs baseline: 1.1614x; 1.1614x over previous
import math
import contextlib
import numpy as np
import ml_dtypes
import concourse.bass as bass
import concourse.mybir as mybir
from concourse.bass_utils import run_bass_kernel_spmd

F32 = mybir.dt.float32
BF16 = mybir.dt.bfloat16
I32 = mybir.dt.int32
AF = mybir.ActivationFunctionType
ALU = mybir.AluOpType
AX = mybir.AxisListType.X
E_SP = mybir.EngineType.SP

BF = ml_dtypes.bfloat16

D = 4096
S = 8192
NCORES = 8
TPC = 1024
TT = 512
NTT = 2
NCH = 32
HD = 128
KVH = 8
NSLOT = 4
NQG = 16
SCALE = 1.0 / math.sqrt(128.0)
SQHD = math.sqrt(128.0)
EPS = 1e-5
WB_A = (1, 3, 9, NQG)
WB_B = (NQG, NQG, NQG, NQG)
PHASES = 4
SUBSTAGE = 9

DT_SZ = {F32: 4, BF16: 2, I32: 4}


def build_program():
    nc = bass.Bass("TRN2", target_bir_lowering=False)

    def din(name, shape, dt):
        return nc.dram_tensor(name, shape, dt, kind="ExternalInput").ap()

    x0 = din("x0", [D, TPC], F32)
    wq_t = din("wq_t", [128, 32 * 3 * D], F32)
    bqkv = din("bqkv", [128, 96], F32)
    wo_t = din("wo_t", [128, 32 * D], F32)
    bout = din("bout", [128, 32], F32)
    wb_t = din("wb_t", [128, 32 * D], F32)
    bbase = din("bbase", [128, 32], F32)
    wr_t = din("wr_t", [D, 8], F32)
    wa_t = din("wa_t", [128, 32 * 128], F32)
    wb2 = din("wb2", [128, D], F32)
    qcod = din("qcod", [128, 32 * D], I32)
    kcod = din("kcod", [128, 32 * KVH * HD], I32)
    vcod = din("vcod", [128, 32 * KVH * HD], I32)
    ocod = din("ocod", [128, 32 * D], I32)
    qscl = din("qscl", [32, D], F32)
    kscl = din("kscl", [32, KVH * HD], F32)
    vscl = din("vscl", [32, KVH * HD], F32)
    oscl = din("oscl", [32, D], F32)
    ropeA = din("ropeA", [128, TPC], BF16)
    ropeB = din("ropeB", [128, TPC], BF16)
    ident = din("ident", [128, 128], BF16)
    identf = din("identf", [128, 128], F32)
    onesr = din("onesr", [1, 128], BF16)
    onesrf = din("onesrf", [1, 128], F32)
    onesc = din("onesc", [128, 1], BF16)
    ones512 = din("ones512", [1, TT], BF16)
    tri = din("tri", [128, 4 * TT], BF16)
    epsap = din("epsap", [1, 1], F32)
    iota128f = din("iota128f", [1, 128], F32)
    sloper = din("sloper", [1, 256], F32)
    baser = din("baser", [1, 256], F32)
    negaq = din("negaq", [4, S], BF16)
    expand = din("expand", [8, 128], BF16)

    xout = nc.dram_tensor("xout", [D, TPC], F32, kind="ExternalOutput").ap()

    def dint(name, shape, dt, shared=False):
        return nc.dram_tensor(name, shape, dt,
                              addr_space=("Shared" if shared else "Local")).ap()

    agin_a = dint("agin_a", [3 * D, TPC], BF16)
    agout_a = dint("agout_a", [NCORES * 3 * D, TPC], BF16, shared=True)
    agin_b = dint("agin_b", [D + 2 * KVH * HD, TPC], BF16)
    agout_b = dint("agout_b", [NCORES * (D + 2 * KVH * HD), TPC], BF16, shared=True)
    q_loc = dint("q_loc", [NSLOT * HD, S], BF16)
    k_loc = dint("k_loc", [NSLOT * HD, S], BF16)
    v_loc = dint("v_loc", [S, NSLOT * HD], BF16)
    q_locb = dint("q_locb", [NSLOT * HD, S], BF16)
    k_locb = dint("k_locb", [HD, S], BF16)
    v_locb = dint("v_locb", [S, HD], BF16)
    ao_loc_a = dint("ao_loc_a", [TPC, D], BF16)
    ao_loc_b = dint("ao_loc_b", [TPC, D], BF16)
    aoin_a = dint("aoin_a", [S, NSLOT * HD], BF16)
    aoout_a = dint("aoout_a", [NCORES * S, NSLOT * HD], BF16, shared=True)
    aoin_b = dint("aoin_b", [S, NSLOT * HD], BF16)
    aoout_b = dint("aoout_b", [NCORES * S, NSLOT * HD], BF16, shared=True)
    wcacheA = dint("wcacheA", [128, 140 * 4096], BF16)
    wcacheB = dint("wcacheB", [128, 140 * 4096], BF16)

    SBUF_BASE = 16512
    SBUF_LIMIT = 229056
    _off = [SBUF_BASE]

    def place(name, shape, dt):
        per = DT_SZ[dt]
        for s_ in shape[1:]:
            per *= s_
        t = nc.alloc_sbuf_tensor_at(name, shape, dt, offset=_off[0])
        _off[0] += (per + 31) // 32 * 32
        assert _off[0] <= SBUF_LIMIT, f"sbuf overflow at {name}: {_off[0]}"
        return t

    x_sb = place("x_sb", [128, NCH * TPC], F32)
    xn_sb = place("xn_sb", [128, NCH * TT], BF16)
    c_ident = place("c_ident", [128, 128], BF16)
    c_identf = place("c_identf", [128, 128], F32)
    c_onesr = place("c_onesr", [1, 128], BF16)
    c_onesrf = place("c_onesrf", [1, 128], F32)
    c_onesc = place("c_onesc", [128, 1], BF16)
    c_bqkv = place("c_bqkv", [128, 96], F32)
    c_bout = place("c_bout", [128, 32], F32)
    c_bbase = place("c_bbase", [128, 32], F32)
    c_eps = place("c_eps", [1, 1], F32)
    c_expand = place("c_expand", [8, 128], BF16)
    c_neg4 = place("c_neg4", [128, 1], F32)
    ln_xb = [place("ln_xb0", [128, TT], BF16)]
    ln_xq = [place("ln_xq0", [128, TT], BF16)]
    ln_rows = place("ln_rows", [1, 2 * TT], F32)
    ln_tapp = place("ln_tapp", [128, TT], F32)
    SCR0 = _off[0]

    class Arena:
        def __init__(self, tag, base=SCR0):
            self.tag = tag
            self.off = base

        def get(self, name, shape, dt):
            per = DT_SZ[dt]
            for s_ in shape[1:]:
                per *= s_
            end = self.off + (per + 31) // 32 * 32
            assert end <= SBUF_LIMIT, f"sbuf overflow {self.tag}{name}: {end}"
            t = nc.alloc_sbuf_tensor_at(self.tag + name, shape, dt, offset=self.off)
            self.off = end
            return t

    PB = [nc.alloc_psum_tensor(f"pb{i}", [128, 512], F32) for i in range(8)]

    def pb_bf(i):
        return PB[i].ap().bitcast(BF16)

    stack = contextlib.ExitStack()
    SEMNAMES = ["ini", "w0", "w1", "cvt", "pe", "ve", "se",
                "kv0", "kv1", "qp0", "qp1", "sto", "ext", "cc", "fin",
                "st0", "st1", "vsq", "ao0", "ao1", "sr",
                "wc0", "wc1", "cr0", "cr1"]
    SEM = {n: stack.enter_context(nc.semaphore(n)) for n in SEMNAMES}
    CNT = {n: 0 for n in SEMNAMES}
    ENGS = ("sync", "scalar", "vector", "tensor", "gpsimd")
    PLAN = {e: [] for e in ENGS}

    _SELF = {"vector": "ve", "scalar": "se", "gpsimd": "cvt"}

    def E(eng, meth, *args, w=(), i=(), lazy=None, **kw):
        w = list(w)
        if meth is not None and eng in _SELF:
            w.append((_SELF[eng], CNT[_SELF[eng]]))
        waits = [(SEM[s], v) for (s, v) in w if v > 0]
        assert not (meth is None and i), "inc on a None op never fires"
        incs = []
        for (s_, n_) in i:
            CNT[s_] += n_
            incs.append((SEM[s_], n_))

        def run(h, _m=meth, _a=args, _k=kw, _w=waits, _i=incs, _lz=lazy):
            for (sem, val) in _w:
                h.wait_ge(sem, val)
            if _m is None:
                return
            a = list(_a)
            k2 = dict(_k)
            if _lz is not None:
                k2["in_"] = _lz(h)[0]
            inst = getattr(h, _m)(*a, **k2)
            for (sem, n_) in _i:
                inst = inst.then_inc(sem, n_)
        PLAN[eng].append(run)

    def cur(s):
        return (s, CNT[s])

    def t_mm(out, lhsT, rhs, start, stop, w=(), i=()):
        E("tensor", "matmul", out, lhsT, rhs, start=start, stop=stop, w=w, i=i)

    def barrier():
        snap = [cur(s) for s in ("pe", "ve", "se", "cvt", "sto", "ext", "cc",
                                 "w0", "w1", "kv0", "kv1", "qp0", "qp1")]
        for eng in ENGS:
            E(eng, None, w=snap)

    E("sync", "dma_start", out=x_sb[:].rearrange("p (c t) -> p c t", t=TPC),
      in_=x0.rearrange("(c p) t -> p c t", p=128), i=[("ini", 16)])
    for (dst, src) in [(c_ident, ident), (c_identf, identf), (c_onesr, onesr),
                       (c_onesrf, onesrf), (c_onesc, onesc),
                       (c_bqkv, bqkv), (c_bout, bout), (c_bbase, bbase),
                       (c_eps, epsap), (c_expand, expand)]:
        E("sync", "dma_start", out=dst[:], in_=src, i=[("ini", 16)])
    for eng in ENGS:
        E(eng, None, w=[cur("ini")])
    E("vector", "memset", c_neg4[:], -4.0, i=[("ve", 1)])


    def ln_into_xn(tt):
        for c in range(NCH):
            xc = x_sb[:, c * TPC + tt * TT: c * TPC + tt * TT + TT]
            wv = [cur("pe")] if c >= 1 else []
            E("vector", "tensor_copy", out=ln_xb[0][:], in_=xc, w=wv, i=[("ve", 1)])
            E("scalar", "activation", out=ln_xq[0][:], in_=xc, func=AF.Square,
              w=wv, i=[("se", 1)])
            t_mm(PB[2][:1, :], c_onesc[:], ln_xb[0][:], start=(c == 0),
                 stop=(c == NCH - 1), w=[cur("ve"), cur("se")], i=[("pe", 1)])
            t_mm(PB[3][:1, :], c_onesc[:], ln_xq[0][:], start=(c == 0),
                 stop=(c == NCH - 1), i=[("pe", 1)])
        r_m = ln_rows[:, 0:TT]
        r_t = ln_rows[:, TT:2 * TT]
        E("vector", "tensor_scalar", w=[cur("pe")], i=[("ve", 1)],
          out=r_m, in0=PB[2][:1, :], scalar1=1.0 / D, scalar2=None, op0=ALU.mult)
        E("vector", "tensor_scalar", i=[("ve", 1)],
          out=r_t, in0=PB[3][:1, :], scalar1=1.0 / D, scalar2=None, op0=ALU.mult)
        E("vector", "scalar_tensor_tensor", out=r_t, in0=r_m, scalar=-1.0,
          in1=r_m, op0=ALU.mult, op1=ALU.mult, i=[("ve", 1)], accum_out=None)
        E("vector", "tensor_scalar", i=[("ve", 1)],
          out=ln_tapp[:1, 0:TT], in0=PB[3][:1, :], scalar1=1.0 / D, scalar2=None,
          op0=ALU.mult)
        E("vector", "tensor_add", out=r_t, in0=r_t, in1=ln_tapp[:1, 0:TT],
          i=[("ve", 1)])
        E("scalar", "activation", out=r_t, in_=r_t, func=AF.Sqrt, bias=c_eps[:],
          w=[cur("ve")], i=[("se", 1)])
        E("vector", "reciprocal", out=r_t, in_=r_t, w=[cur("se")], i=[("ve", 1)])
        E("vector", "scalar_tensor_tensor", out=r_m, in0=r_m, scalar=-1.0,
          in1=r_t, op0=ALU.mult, op1=ALU.mult, i=[("ve", 1)])
        t_mm(PB[2][:, :], c_onesrf[:], r_t, start=True, stop=True,
             w=[cur("ve")], i=[("pe", 1)])
        t_mm(PB[3][:, :], c_onesrf[:], r_m, start=True, stop=True, i=[("pe", 1)])
        for c in range(NCH):
            xc = x_sb[:, c * TPC + tt * TT: c * TPC + tt * TT + TT]
            xnc = xn_sb[:, c * TT:(c + 1) * TT]
            E("vector", "tensor_mul", out=ln_tapp[:], in0=xc, in1=PB[2][:, :],
              w=[cur("pe")] if c == 0 else [], i=[("ve", 1)])
            E("vector", "tensor_add", out=xnc, in0=ln_tapp[:], in1=PB[3][:, :],
              i=[("ve", 1)])
        E("vector", "engine_nop", i=[("ve", 1)])
        return cur("ve")

    def make_wmach(ar, tag):
        m = {
            "wst": [ar.get(tag + "wst0", [128, 1024], F32),
                    ar.get(tag + "wst1", [128, 1024], F32)],
            "wbf": [ar.get(tag + "wbf0", [128, 4096], BF16),
                    ar.get(tag + "wbf1", [128, 4096], BF16)],
            "wst_free": [0, 0],
            "wbf_free": [0, 0],
            "hidx": 0,
        }
        return m

    WCOFF = [0]
    WCMAP = {}

    def cache_slot(key):
        if key in WCMAP:
            return WCMAP[key], True
        n = WCOFF[0]
        WCOFF[0] += 1
        slot = (wcacheA if n < 140 else wcacheB, (n % 140) * 4096)
        WCMAP[key] = slot
        return slot, False

    def cache_write(wbf, par, ready):
        ct, off = cache_write.off
        E("sync", "dma_start", out=ct[:, off:off + 4096],
          in_=wbf[:], w=[ready], i=[("wc%d" % par, 16)])

    def cache_read(m, par, slot):
        ct, off = slot
        sem = "cr%d" % par
        E("scalar", "dma_start", out=m["wbf"][par][:],
          in_=ct[:, off:off + 4096],
          w=[("pe", m["wbf_free"][par]), cur("wc0"), cur("wc1"),
             ("ve", m.setdefault("sc_done", [0, 0])[par])],
          i=[(sem, 16)])
        return (sem, CNT[sem])

    def wload(m, wsrc, oc, par, kmajor=True, dst=None):
        if dst is None:
            dst = m["wbf"][par]
            off, hit = cache_slot((wsrc.tensor.name, oc))
            if hit:
                return cache_read(m, par, off)
        else:
            off = None
        for h in range(4):
            b = m["hidx"] % 2
            m["hidx"] += 1
            sem = "w%d" % b
            if kmajor:
                src = wsrc[:, (oc * 4 + h) * 1024:(oc * 4 + h + 1) * 1024]
            else:
                src = wsrc[:, h * 1024:(h + 1) * 1024]
            dst_st = m["wst"][b][:]
            E("scalar", "dma_start", out=dst_st, in_=src,
              w=[("cvt", m["wst_free"][b])], i=[(sem, 16)])
            if kmajor:
                dcv = dst[:].rearrange("p (kc o) -> p kc o", o=128)[:, :, h * 32:(h + 1) * 32]
                scv = m["wst"][b][:].rearrange("p (kc o) -> p kc o", o=32)
            else:
                dcv = dst[:, h * 1024:(h + 1) * 1024]
                scv = m["wst"][b][:]
            cvw = [(sem, CNT[sem]), ("pe", m["wbf_free"][par])]
            if h == 0 and off is not None:
                cvw += [("wc%d" % par, m.setdefault("wc_free", [0, 0])[par])]
            E("gpsimd", "tensor_copy", out=dcv, in_=scv, w=cvw, i=[("cvt", 1)])
            m["wst_free"][b] = CNT["cvt"]
        if off is not None:
            cache_write.off = off
            cache_write(dst, par, cur("cvt"))
            m["wc_free"][par] = CNT["wc%d" % par]
        return cur("cvt")

    def proj_matmuls(m, par, pbank, cvt_ready, xn_ready, pb_free_ve, last_stop,
                     rhs=None, i=()):
        rb = xn_sb if rhs is None else rhs
        for kc in range(NCH):
            wv = []
            if kc == 0:
                wv = [cvt_ready, xn_ready, ("ve", pb_free_ve)]
            t_mm(PB[pbank][:, :], m["wbf"][par][:, kc * 128:(kc + 1) * 128],
                 rb[:, kc * TT:(kc + 1) * TT], start=(kc == 0),
                 stop=(last_stop and kc == NCH - 1), w=wv,
                 i=(i if kc == NCH - 1 else ()))
        m["wbf_free"][par] = CNT["pe"]

    arA = Arena("A_")
    wmA = make_wmach(arA, "")
    stgA = [arA.get("stg0", [128, TT], BF16), arA.get("stg1", [128, TT], BF16)]
    sqA = arA.get("sq", [128, TT], BF16)
    stg_free = [0, 0]
    pb_free = [0, 0]

    def v_transpose_store(stg, sq, vview, tt, h, tag_free):
        for q4 in range(4):
            bb = 6 + q4 % 2
            E("tensor", "transpose", out=pb_bf(bb)[:, 0:128],
              in_=stg[:, q4 * 128:(q4 + 1) * 128], identity=c_ident[:],
              w=[cur("ve")], i=[("pe", 1)])
            E("vector", "tensor_copy", out=sq[:, q4 * 128:(q4 + 1) * 128],
              in_=pb_bf(bb)[:, 0:128], w=[cur("pe"), cur("vsq")], i=[("ve", 1)])
        E("sync", "dma_start",
          out=vview[tt * TT:(tt + 1) * TT, h * 128:(h + 1) * 128]
              .rearrange("(qs p) v -> p qs v", p=128),
          in_=sq[:].rearrange("p (qs v) -> p qs v", v=128),
          w=[cur("ve")], i=[("vsq", 16)])

    def attn_a_proj(tt):
        xn_ready = ln_into_xn(tt)
        vview = agin_a[2 * D:3 * D, :].rearrange("(a b) c -> a (b c)", b=4)
        order = list(range(32, 96)) + list(range(0, 32))
        for n, oc in enumerate(order):
            par = n % 2
            cvt_ready = wload(wmA, wq_t, oc, par)
            proj_matmuls(wmA, par, par, cvt_ready, xn_ready, pb_free[par],
                         last_stop=True, i=[("pe", 1)])
            stg = stgA[par]
            ssem = "st%d" % par
            E("vector", "tensor_scalar", out=stg[:], in0=PB[par][:, :],
              scalar1=c_bqkv[:, oc:oc + 1], scalar2=None, op0=ALU.add,
              w=[cur("pe"), (ssem, stg_free[par])], i=[("ve", 1)])
            pb_free[par] = CNT["ve"]
            h = oc % 32
            if 32 <= oc < 64:
                E("sync", "dma_start",
                  out=agin_a[D + h * 128: D + (h + 1) * 128, tt * TT:(tt + 1) * TT],
                  in_=stg[:], w=[cur("ve")], i=[(ssem, 16)])
                stg_free[par] = CNT[ssem]
            elif oc >= 64:
                v_transpose_store(stg, sqA, vview, tt, h, None)
                stg_free[par] = 0
            else:
                E("sync", "dma_start",
                  out=agin_a[h * 128:(h + 1) * 128, tt * TT:(tt + 1) * TT],
                  in_=stg[:], w=[cur("ve")], i=[(ssem, 16)])
                stg_free[par] = CNT[ssem]

    arF = Arena("F_")
    f_q = [arF.get("qb0", [128, TT], BF16), arF.get("qb1", [128, TT], BF16)]
    f_k = [arF.get("kb0", [128, TT], BF16), arF.get("kb1", [128, TT], BF16)]
    f_v = [arF.get("vb0", [128, 4 * 132], BF16), arF.get("vb1", [128, 4 * 132], BF16)]
    f_pt = [arF.get("pt0", [128, 4 * TT], BF16), arF.get("pt1", [128, 4 * TT], BF16)]
    f_tri = arF.get("tri", [128, 4 * TT], BF16)
    f_alc = arF.get("alc", [128, 256], F32)
    f_io = arF.get("io", [1, 128], F32)
    f_sl = arF.get("sl", [1, 256], F32)
    f_ba = arF.get("ba", [1, 256], F32)
    f_nb = [arF.get("nb0", [1, TT], BF16), arF.get("nb1", [1, TT], BF16)]
    f_nbf = arF.get("nbf", [1, TT], BF16)
    f_qsq = arF.get("qsq", [128, TT], BF16)
    f_ksq = arF.get("ksq", [128, TT], BF16)
    f_kn = arF.get("kn", [1, TT], F32)
    f_sr = arF.get("sr", [1, TT], BF16)
    f_scl = arF.get("scl", [1, 16], F32)
    f_colc = arF.get("colc", [1, 1], BF16)
    f_csc = arF.get("csc", [1, 1], BF16)
    f_rc = arF.get("rc", [128, 4], F32)
    f_ao = [arF.get("ao", [128, 512], BF16), arF.get("ao1", [128, 512], BF16)]
    f_ones512 = arF.get("o512", [1, TT], BF16)

    def flash_tables(alibi):
        E("sync", "dma_start", out=f_tri[:], in_=tri, i=[("ext", 16)])
        E("sync", "dma_start", out=f_ones512[:], in_=ones512, i=[("ext", 16)])
        if alibi:
            E("sync", "dma_start", out=f_io[:], in_=iota128f, i=[("ext", 16)])
            E("sync", "dma_start", out=f_sl[:], in_=sloper, i=[("ext", 16)])
            E("sync", "dma_start", out=f_ba[:], in_=baser, i=[("ext", 16)])
        for par in range(2):
            E("vector", "memset", f_v[par][:], 0.0, i=[("ve", 1)])
            for kc in range(4):
                E("vector", "memset", f_v[par][:, kc * 132 + 128:kc * 132 + 129],
                  1.0, i=[("ve", 1)])
        snap = [cur("ext"), cur("ve")]
        for eng in ("tensor", "scalar", "vector", "sync"):
            E(eng, None, w=snap)
        if alibi:
            for piece in range(4):
                bank = piece % 2
                sl_ = slice(piece * 64, (piece + 1) * 64)
                t_mm(PB[bank][:, 0:64], f_io[:], f_sl[:, sl_], start=True,
                     stop=False, w=[cur("ve")], i=())
                t_mm(PB[bank][:, 0:64], c_onesrf[:], f_ba[:, sl_], start=False,
                     stop=True, i=[("pe", 1)])
                E("vector", "tensor_copy", out=f_alc[:, sl_], in_=PB[bank][:, 0:64],
                  w=[cur("pe")], i=[("ve", 1)])

    def posmap(j):
        o = j if j < 8 else 15 - j
        lb = 0 if j < 8 else 1
        return o * 1024 + lb * 512

    def emit_flash(cfg):
        alibi = cfg["alibi"]
        WB = cfg["WB"]
        qloc, kloc, vloc = cfg["qloc"], cfg["kloc"], cfg["vloc"]
        ao_in = cfg["ao_in"]
        kvsh = cfg["kv_shared"]
        flash_tables(alibi)
        expdone = [0, 0, 0, 0]
        qkdone = [0, 0, 0, 0]
        avdone = [0, 0]
        podone = 0
        qfree = [0, 0]
        tile_i = [0]

        def kv_load(s, j, par):
            sem = "kv%d" % par
            krow = 0 if kvsh else s * HD
            jc = posmap(j)
            E("sync", "dma_start", out=f_k[par][:],
              in_=kloc[krow:krow + HD, jc:jc + TT],
              w=[("pe", avdone[par])], i=[(sem, 16)])
            E("sync", "dma_start",
              out=f_v[par][:].rearrange("p (kc v) -> p kc v", v=132)[:, :, 0:128],
              in_=vloc[jc:jc + TT, krow:krow + HD]
                  .rearrange("(kc p) v -> p kc v", p=128),
              i=[(sem, 16)])
            return (sem, CNT[sem])

        for s in range(NSLOT):
            if (not kvsh) or s == 0:
                krow = 0 if kvsh else s * HD
                kn_free = [0, 0]
                knmm = 0
                knve = podone
                for piece in range(NQG):
                    par = piece % 2
                    sem = "kv%d" % par
                    E("sync", "dma_start", out=f_k[par][:],
                      in_=kloc[krow:krow + HD, posmap(piece):posmap(piece) + TT],
                      w=[("pe", avdone[par]), ("se", kn_free[par])],
                      i=[(sem, 16)])
                    E("scalar", "activation", out=f_ksq[:], in_=f_k[par][:],
                      func=AF.Square, w=[(sem, CNT[sem]), ("pe", knmm)],
                      i=[("se", 1)])
                    kn_free[par] = CNT["se"]
                    t_mm(PB[2][:1, 0:TT], c_onesc[:], f_ksq[:], start=True,
                         stop=True, w=[cur("se"), ("ve", knve),
                                       ("se", expdone[2])], i=[("pe", 1)])
                    knmm = CNT["pe"]
                    E("vector", "tensor_copy", out=f_kn[:], in_=PB[2][:1, 0:TT],
                      w=[cur("pe")], i=[("ve", 1)])
                    knve = CNT["ve"]
                    if piece == 0:
                        E("vector", "reduce_sum", out=f_scl[:, 0:1], in_=f_kn[:],
                          axis=AX, i=[("ve", 1)])
                        E("vector", "reduce_max", out=f_scl[:, 1:2], in_=f_kn[:],
                          axis=AX, i=[("ve", 1)])
                    else:
                        E("vector", "reduce_max", out=f_scl[:, 2:3], in_=f_kn[:],
                          axis=AX, i=[("ve", 1)])
                        E("vector", "tensor_max", out=f_scl[:, 1:2],
                          in0=f_scl[:, 1:2], in1=f_scl[:, 2:3], i=[("ve", 1)])
                E("sync", None, w=[cur("se")])
                E("vector", "tensor_mul", out=f_scl[:, 3:4], in0=f_scl[:, 0:1],
                  in1=f_scl[:, 1:2], i=[("ve", 1)])
                E("vector", "tensor_scalar", out=f_scl[:, 3:4], in0=f_scl[:, 3:4],
                  scalar1=1.0 / TT, scalar2=None, op0=ALU.mult, i=[("ve", 1)])
                E("scalar", "activation", out=f_scl[:, 4:5], in_=f_scl[:, 3:4],
                  func=AF.Sqrt, w=[cur("ve")], i=[("se", 1)])
                E("vector", "reciprocal", out=f_scl[:, 5:6], in_=f_scl[:, 0:1],
                  w=[cur("se")], i=[("ve", 1)])
                E("vector", "tensor_mul", out=f_scl[:, 6:7], in0=f_scl[:, 1:2],
                  in1=f_scl[:, 5:6], i=[("ve", 1)])
                E("vector", "tensor_scalar", out=f_scl[:, 6:7], in0=f_scl[:, 6:7],
                  scalar1=float(TT), scalar2=None, op0=ALU.mult, i=[("ve", 1)])
                E("scalar", "activation", out=f_scl[:, 7:8], in_=f_scl[:, 6:7],
                  func=AF.Sqrt, w=[cur("ve")], i=[("se", 1)])
                E("vector", "tensor_scalar", out=f_colc[:], in0=f_scl[:, 7:8],
                  scalar1=-0.5, scalar2=None, op0=ALU.mult, w=[cur("se")],
                  i=[("ve", 1)])
                E("vector", "tensor_scalar", out=f_csc[:], in0=f_scl[:, 4:5],
                  scalar1=-0.5, scalar2=None, op0=ALU.mult, i=[("ve", 1)])

            for qg in range(NQG):
                qpar = qg % 2
                qsem = "qp%d" % qpar
                E("sync", "dma_start", out=f_q[qpar][:],
                  in_=qloc[s * HD:(s + 1) * HD, posmap(qg):posmap(qg) + TT],
                  w=[("pe", qfree[qpar])], i=[(qsem, 16)])
                if alibi:
                    E("sync", "dma_start", out=f_nb[qpar][:],
                      in_=cfg["negaq"][s:s + 1, qg * TT:(qg + 1) * TT],
                      i=[(qsem, 16)])
                qready = (qsem, CNT[qsem])
                E("scalar", "activation", out=f_qsq[:], in_=f_q[qpar][:],
                  func=AF.Square, w=[qready], i=[("se", 1)])
                t_mm(PB[2][:1, 0:TT], c_onesc[:], f_qsq[:], start=True, stop=True,
                     w=[cur("se"), ("se", expdone[2]), ("ve", podone)], i=[("pe", 1)])
                E("vector", "tensor_copy", out=f_sr[:], in_=PB[2][:1, 0:TT],
                  w=[cur("pe")], i=[("ve", 1)])
                t_mm(PB[3][:1, 0:TT], f_colc[:], f_sr[:], start=True, stop=False,
                     w=[cur("ve"), ("se", expdone[3])], i=())
                if alibi:
                    t_mm(PB[3][:1, 0:TT], f_csc[:], f_ones512[:], start=False,
                         stop=False, i=())
                    t_mm(PB[3][:1, 0:TT], c_onesr[:, 0:1], f_nb[qpar][:],
                         start=False, stop=True, w=[qready], i=[("pe", 1)])
                else:
                    t_mm(PB[3][:1, 0:TT], f_csc[:], f_ones512[:], start=False,
                         stop=True, i=[("pe", 1)])
                E("vector", "tensor_copy", out=f_nbf[:], in_=PB[3][:1, 0:TT],
                  w=[cur("pe")], i=[("ve", 1)])
                nbready = cur("ve")

                jlo = max(0, qg - WB[s])
                js = list(range(jlo, qg + 1))
                kvready = {}
                kvready[js[0]] = kv_load(s, js[0], tile_i[0] % 2)
                first = True
                for jn, j in enumerate(js):
                    par = tile_i[0] % 2
                    if jn + 1 < len(js):
                        kvready[js[jn + 1]] = kv_load(s, js[jn + 1], (tile_i[0] + 1) % 2)
                    for kc in range(4):
                        sb = kc
                        t_mm(PB[sb][:, 0:TT], c_onesr[:], f_nbf[:],
                             start=True, stop=False,
                             w=[kvready[j], qready, ("se", expdone[sb]),
                                ("ve", nbready[1])], i=())
                        t_mm(PB[sb][:, 0:TT], f_k[par][:, kc * 128:(kc + 1) * 128],
                             f_q[qpar][:], start=False, stop=True, i=[("pe", 1)])
                        qkdone[sb] = CNT["pe"]
                        expw = [("pe", qkdone[sb]), ("pe", avdone[par])]
                        if j == qg:
                            E("vector", "tensor_tensor", out=PB[sb][:, 0:TT],
                              in0=PB[sb][:, 0:TT],
                              in1=f_tri[:, kc * TT:(kc + 1) * TT], op=ALU.min,
                              w=[("pe", qkdone[sb])], i=[("ve", 1)])
                            expw.append(("ve", CNT["ve"]))
                        if alibi:
                            bias = f_alc[:, s * 64 + j * 4 + kc: s * 64 + j * 4 + kc + 1]
                            E("scalar", "activation",
                              out=f_pt[par][:, kc * TT:(kc + 1) * TT],
                              in_=PB[sb][:, 0:TT], func=AF.Exp, bias=bias,
                              scale=SCALE, w=expw, i=[("se", 1)])
                        else:
                            E("scalar", "activation",
                              out=f_pt[par][:, kc * TT:(kc + 1) * TT],
                              in_=PB[sb][:, 0:TT], func=AF.Exp, scale=SCALE,
                              w=expw, i=[("se", 1)])
                        expdone[sb] = CNT["se"]
                    for kc in range(4):
                        for qs in range(4):
                            wv = []
                            if kc == 0 and qs == 0:
                                wv = [cur("se")]
                                if first:
                                    wv.append(("ve", podone))
                                wv.append(kvready[j])
                            last_av_kb = (kc == 3 and qs == 3)
                            t_mm(PB[4 + qs][:, 0:129],
                                 f_pt[par][:, kc * TT + qs * 128: kc * TT + (qs + 1) * 128],
                                 f_v[par][:, kc * 132:kc * 132 + 129],
                                 start=(first and kc == 0), stop=(j == js[-1] and kc == 3),
                                 w=wv, i=([("pe", 1)] if last_av_kb else ()))
                    avdone[par] = CNT["pe"]
                    first = False
                    tile_i[0] += 1
                qfree[qpar] = CNT["pe"]
                aop = qg % 2
                aosem = "ao%d" % aop
                for qs in range(4):
                    E("vector", "reciprocal", out=f_rc[:, qs:qs + 1],
                      in_=PB[4 + qs][:, 128:129],
                      w=[cur("pe")] if qs == 0 else [], i=[("ve", 1)])
                    E("vector", "tensor_scalar",
                      out=f_ao[aop][:, qs * 128:(qs + 1) * 128],
                      in0=PB[4 + qs][:, 0:128], scalar1=f_rc[:, qs:qs + 1],
                      scalar2=None, op0=ALU.mult,
                      w=[(aosem, CNT[aosem])] if qs == 0 else [], i=[("ve", 1)])
                podone = CNT["ve"]
                E("sync", "dma_start",
                  out=ao_in[qg * TT:(qg + 1) * TT, s * HD:(s + 1) * HD]
                      .rearrange("(qs p) h -> p qs h", p=128),
                  in_=f_ao[aop][:].rearrange("p (qs h) -> p qs h", h=128),
                  w=[cur("ve")], i=[(aosem, 16)])

    def extract_a():
        BW = 3 * D

        def run_sw(h):
            for arm in h.switch_core_id(n=128):
                c = arm.logical % 8
                for part, dst in ((0, q_loc), (1, k_loc)):
                    for cb in range(2):
                        for sl in range(NSLOT):
                            src = bass.AP(
                                agout_a.tensor,
                                (BW * 0 + part * D + (8 * sl + c) * 128) * TPC + cb * 512,
                                [[BW * TPC, NCORES], [TPC, 128], [1, TT]])
                            dstap = bass.AP(
                                dst.tensor,
                                sl * 128 * S + cb * 512,
                                [[1024, NCORES], [S, 128], [1, TT]])
                            h.dma_start(out=dstap, in_=src).then_inc(SEM["ext"], 16)
                for cb in range(2):
                    for sl in range(NSLOT):
                        src = bass.AP(
                            agout_a.tensor,
                            2 * D * TPC + cb * 512 * D + (8 * sl + c) * 128,
                            [[BW * TPC, NCORES], [D, TT], [1, 128]])
                        dstap = bass.AP(
                            v_loc.tensor,
                            cb * 512 * 512 + sl * 128,
                            [[TPC * 512, NCORES], [512, TT], [1, 128]])
                        h.dma_start(out=dstap, in_=src).then_inc(SEM["ext"], 16)
        PLAN["sync"].append(run_sw)
        CNT["ext"] += 16 * 24
        E("sync", None, w=[cur("ext")])
        for eng in ("tensor", "scalar", "vector"):
            E(eng, None, w=[cur("ext")])

    def extract_b():
        BWB = D + 2 * KVH * HD

        def run_sw(h):
            for arm in h.switch_core_id(n=128):
                c = arm.logical % 8
                for cb in range(2):
                    for sl in range(NSLOT):
                        src = bass.AP(
                            agout_b.tensor,
                            ((4 * c + sl) * 128) * TPC + cb * 512,
                            [[BWB * TPC, NCORES], [TPC, 128], [1, TT]])
                        dstap = bass.AP(
                            q_locb.tensor,
                            sl * 128 * S + cb * 512,
                            [[1024, NCORES], [S, 128], [1, TT]])
                        h.dma_start(out=dstap, in_=src).then_inc(SEM["ext"], 16)
                    src = bass.AP(
                        agout_b.tensor,
                        (D + 128 * c) * TPC + cb * 512,
                        [[BWB * TPC, NCORES], [TPC, 128], [1, TT]])
                    dstap = bass.AP(
                        k_locb.tensor, cb * 512,
                        [[1024, NCORES], [S, 128], [1, TT]])
                    h.dma_start(out=dstap, in_=src).then_inc(SEM["ext"], 16)
                    src = bass.AP(
                        agout_b.tensor,
                        (D + KVH * HD) * TPC + cb * 512 * KVH * HD + 128 * c,
                        [[BWB * TPC, NCORES], [KVH * HD, TT], [1, 128]])
                    dstap = bass.AP(
                        v_locb.tensor, cb * 512 * HD,
                        [[TPC * HD, NCORES], [HD, TT], [1, 128]])
                    h.dma_start(out=dstap, in_=src).then_inc(SEM["ext"], 16)
        PLAN["sync"].append(run_sw)
        CNT["ext"] += 16 * 12
        E("sync", None, w=[cur("ext")])
        for eng in ("tensor", "scalar", "vector"):
            E(eng, None, w=[cur("ext")])

    def extract_ao(ao_out, ao_loc):
        def run_sw(h):
            for arm in h.switch_core_id(n=128):
                c = arm.logical % 8
                for cb in range(2):
                    tabs = (c if cb == 0 else 15 - c) * 512
                    for sl in range(NSLOT):
                        src = bass.AP(
                            ao_out.tensor,
                            tabs * 512 + sl * 128,
                            [[S * 512, NCORES], [512, TT], [1, 128]])
                        dstap = bass.AP(
                            ao_loc.tensor,
                            cb * 512 * D + sl * 128,
                            [[NSLOT * 128, NCORES], [D, TT], [1, 128]])
                        h.dma_start(out=dstap, in_=src).then_inc(SEM["ext"], 16)
        PLAN["sync"].append(run_sw)
        CNT["ext"] += 16 * 8
        E("sync", None, w=[cur("ext")])
        for eng in ("tensor", "scalar", "vector"):
            E(eng, None, w=[cur("ext")])

    def o_proj(ao_loc, w_src=None, bias=None, dequant=None, wm=None, stg2=None):
        stg_free = [CNT["pe"], CNT["pe"]]
        for tt in range(NTT):
            for cidx in range(32):
                for q4 in range(4):
                    stg = stg2[q4 % 2]
                    sem = "kv%d" % (q4 % 2)
                    E("sync", "dma_start", w=[("pe", stg_free[q4 % 2])],
                      in_=ao_loc[tt * TT + q4 * 128: tt * TT + (q4 + 1) * 128,
                                 cidx * 128:(cidx + 1) * 128],
                      out=stg[:, 0:128], i=[(sem, 16)])
                    bb = 6 + q4 % 2
                    E("tensor", "transpose", out=pb_bf(bb)[:, 0:128],
                      in_=stg[:, 0:128], identity=c_ident[:],
                      w=[(sem, CNT[sem]), cur("ve")], i=[("pe", 1)])
                    stg_free[q4 % 2] = CNT["pe"]
                    E("vector", "tensor_copy",
                      out=xn_sb[:, cidx * TT + q4 * 128: cidx * TT + (q4 + 1) * 128],
                      in_=pb_bf(bb)[:, 0:128], w=[cur("pe")], i=[("ve", 1)])
            E("vector", "engine_nop", i=[("ve", 1)])
            xn_ready = cur("ve")
            opb_free = [0, 0]
            for oc in range(32):
                par = oc % 2
                if dequant is None:
                    cvt_ready = wload(wm, w_src, oc, par)
                else:
                    cvt_ready = dequant(oc, par)
                proj_matmuls(wm, par, par, cvt_ready, xn_ready, opb_free[par],
                             last_stop=True, i=[("pe", 1)])
                xc = x_sb[:, oc * TPC + tt * TT: oc * TPC + tt * TT + TT]
                if bias is not None:
                    E("vector", "scalar_tensor_tensor", out=xc, in0=PB[par][:, :],
                      scalar=bias[:, oc:oc + 1], in1=xc, op0=ALU.add, op1=ALU.add,
                      w=[cur("pe")], i=[("ve", 1)])
                else:
                    E("vector", "tensor_add", out=xc, in0=PB[par][:, :], in1=xc,
                      w=[cur("pe")], i=[("ve", 1)])
                opb_free[par] = CNT["ve"]

    def phase_attn_a():
        for tt in range(NTT):
            attn_a_proj(tt)
        E("gpsimd", "collective_compute", "AllGather", ALU.bypass,
          [list(range(NCORES))], [agin_a.opt()], [agout_a.opt()],
          w=[cur("st0"), cur("st1"), cur("vsq")], i=[("cc", 1)])
        barrier()
        if SUBSTAGE < 2:
            return
        extract_a()
        if SUBSTAGE < 3:
            return
        emit_flash(dict(alibi=True, WB=WB_A, qloc=q_loc, kloc=k_loc,
                        vloc=v_loc, ao_in=aoin_a, negaq=negaq, kv_shared=False))
        if SUBSTAGE < 4:
            return
        E("gpsimd", "collective_compute", "AllGather", ALU.bypass,
          [list(range(NCORES))], [aoin_a.opt()], [aoout_a.opt()],
          w=[cur("ao0"), cur("ao1")], i=[("cc", 1)])
        barrier()
        extract_ao(aoout_a, ao_loc_a)
        arO = Arena("Oa_")
        wmO = make_wmach(arO, "")
        stgO = [arO.get("st0", [128, TT], BF16), arO.get("st1", [128, TT], BF16)]
        o_proj(ao_loc_a, w_src=wo_t, bias=c_bout, wm=wmO, stg2=stgO)
        barrier()

    def phase_ffn(idx):
        arN = Arena("N%d_" % idx)
        wmN = make_wmach(arN, "")
        b2bf = arN.get("b2", [128, 4096], BF16)
        g_lr = arN.get("lr", [8, TT], BF16)
        g_lt = arN.get("lt", [128, 32], F32)
        g_wt = arN.get("wt", [128, 32], BF16)
        g_m = arN.get("m", [128, 4], F32)
        g_msk = arN.get("msk", [128, 8], F32)
        g_l2 = arN.get("l2", [128, 8], F32)
        g_kp = arN.get("kp", [128, 8], F32)
        g_u = arN.get("u", [128, 8], F32)
        g_wf = arN.get("wf", [8, TT], BF16)
        g_we = arN.get("we", [128, TT], BF16)
        g_g = arN.get("g", [128, TT], BF16)
        rbf = arN.get("rbf", [128, 256], BF16)

        wload(wmN, wb2, 0, 0, kmajor=False, dst=b2bf)
        E("scalar", "dma_start", out=wmN["wst"][0][:, 0:256]
          .rearrange("p (kc o) -> p kc o", o=8),
          in_=wr_t.rearrange("(kc p) o -> p kc o", p=128),
          w=[("cvt", wmN["wst_free"][0])], i=[("w0", 16)])
        E("gpsimd", "tensor_copy", out=rbf[:], in_=wmN["wst"][0][:, 0:256],
          w=[cur("w0")], i=[("cvt", 1)])
        wmN["wst_free"][0] = CNT["cvt"]
        abreQ = cur("cvt")

        for tt in range(NTT):
            xn_ready = ln_into_xn(tt)
            for kc in range(NCH):
                t_mm(PB[2][0:8, 0:TT], rbf[:, kc * 8:(kc + 1) * 8],
                     xn_sb[:, kc * TT:(kc + 1) * TT], start=(kc == 0),
                     stop=(kc == NCH - 1),
                     w=([abreQ, xn_ready] if kc == 0 else []),
                     i=([("pe", 1)] if kc == NCH - 1 else ()))
            E("vector", "tensor_copy", out=g_lr[:], in_=PB[2][0:8, 0:TT],
              w=[cur("pe")], i=[("ve", 1)])
            for q4 in range(4):
                E("tensor", "transpose", out=pb_bf(3)[:, 0:8],
                  in_=g_lr[:, q4 * 128:(q4 + 1) * 128], identity=c_ident[0:8, 0:8],
                  w=[cur("ve")], i=[("pe", 1)])
                E("vector", "tensor_copy", out=g_lt[:, q4 * 8:(q4 + 1) * 8],
                  in_=pb_bf(3)[:, 0:8], w=[cur("pe")], i=[("ve", 1)])
            for q4 in range(4):
                lt = g_lt[:, q4 * 8:(q4 + 1) * 8]
                E("vector", "reduce_max", out=g_m[:, 0:1], in_=lt, axis=AX,
                  negate=True, i=[("ve", 1)])
                E("vector", "tensor_scalar", out=g_msk[:], in0=lt,
                  scalar1=g_m[:, 0:1], scalar2=None, op0=ALU.add, i=[("ve", 1)])
                E("vector", "tensor_scalar", out=g_l2[:], in0=g_msk[:],
                  scalar1=0.0, scalar2=None, op0=ALU.is_ge, i=[("ve", 1)])
                E("vector", "scalar_tensor_tensor", out=g_l2[:], in0=g_l2[:],
                  scalar=-1e30, in1=g_msk[:], op0=ALU.mult, op1=ALU.add,
                  i=[("ve", 1)])
                E("vector", "reduce_max", out=g_m[:, 1:2], in_=g_l2[:], axis=AX,
                  negate=True, i=[("ve", 1)])
                E("vector", "tensor_scalar", out=g_kp[:], in0=g_msk[:],
                  scalar1=g_m[:, 1:2], scalar2=None, op0=ALU.add, i=[("ve", 1)])
                E("vector", "tensor_scalar", out=g_kp[:], in0=g_kp[:],
                  scalar1=0.0, scalar2=None, op0=ALU.is_ge, i=[("ve", 1)])
                E("scalar", "activation", out=g_u[:], in_=g_msk[:], func=AF.Exp,
                  w=[cur("ve")], i=[("se", 1)])
                E("vector", "tensor_mul", out=g_u[:], in0=g_u[:], in1=g_kp[:],
                  w=[cur("se")], i=[("ve", 1)])
                E("vector", "reduce_sum", out=g_m[:, 2:3], in_=g_u[:], axis=AX,
                  i=[("ve", 1)])
                E("vector", "reciprocal", out=g_m[:, 2:3], in_=g_m[:, 2:3],
                  i=[("ve", 1)])
                E("vector", "tensor_scalar", out=g_wt[:, q4 * 8:(q4 + 1) * 8],
                  in0=g_u[:], scalar1=g_m[:, 2:3], scalar2=2.0, op0=ALU.mult,
                  op1=ALU.mult, i=[("ve", 1)])
            for q4 in range(4):
                E("tensor", "transpose", out=pb_bf(3)[0:8, 0:128],
                  in_=g_wt[:, q4 * 8:(q4 + 1) * 8], identity=c_ident[:],
                  w=[cur("ve")], i=[("pe", 1)])
                E("vector", "tensor_copy", out=g_wf[:, q4 * 128:(q4 + 1) * 128],
                  in_=pb_bf(3)[0:8, 0:128], w=[cur("pe")], i=[("ve", 1)])
            t_mm(PB[3][:, 0:TT], c_expand[:], g_wf[:], start=True, stop=True,
                 w=[cur("ve")], i=[("pe", 1)])
            E("vector", "tensor_copy", out=g_we[:], in_=PB[3][:, 0:TT],
              w=[cur("pe")], i=[("ve", 1)])
            acvt = wload(wmN, wa_t, 0, 1)
            for kc in range(NCH):
                t_mm(PB[2][:, 0:TT], wmN["wbf"][1][:, kc * 128:(kc + 1) * 128],
                     xn_sb[:, kc * TT:(kc + 1) * TT], start=(kc == 0),
                     stop=(kc == NCH - 1),
                     w=([cur("ve"), acvt] if kc == 0 else []),
                     i=([("pe", 1)] if kc == NCH - 1 else ()))
            wmN["wbf_free"][1] = CNT["pe"]
            E("vector", "tensor_mul", out=g_g[:], in0=PB[2][:, 0:TT], in1=g_we[:],
              w=[cur("pe")], i=[("ve", 1)])
            gready = cur("ve")

            fpb_free = [0, 0]
            for oc in range(32):
                par = oc % 2
                cvt_ready = wload(wmN, wb_t, oc, par)
                proj_matmuls(wmN, par, par, cvt_ready, xn_ready, fpb_free[par],
                             last_stop=False)
                t_mm(PB[par][:, :], b2bf[:, oc * 128:(oc + 1) * 128], g_g[:],
                     start=False, stop=True, w=([gready] if oc < 2 else []),
                     i=[("pe", 1)])
                wmN["wbf_free"][par] = CNT["pe"]
                xc = x_sb[:, oc * TPC + tt * TT: oc * TPC + tt * TT + TT]
                E("vector", "scalar_tensor_tensor", out=xc, in0=PB[par][:, :],
                  scalar=c_bbase[:, oc:oc + 1], in1=xc, op0=ALU.add, op1=ALU.add,
                  w=[cur("pe")], i=[("ve", 1)])
                fpb_free[par] = CNT["ve"]
        barrier()

    def phase_attn_b():
        arB = Arena("B_")
        wmB = make_wmach(arB, "")
        cstI = [wmB["wst"][0].ap().bitcast(I32), wmB["wst"][1].ap().bitcast(I32)]
        srowf = [arB.get("sf0", [1, TT], F32)]
        srow = [arB.get("sr0", [1, TT], BF16), arB.get("sr1", [1, TT], BF16)]
        sbb = [arB.get("sb0", [128, TT], BF16), arB.get("sb1", [128, TT], BF16)]
        ropA = arB.get("rA", [128, TT], BF16)
        ropB = arB.get("rB", [128, TT], BF16)
        rt1 = arB.get("rt1", [128, TT], BF16)
        rt2 = arB.get("rt2", [128, TT], BF16)
        stgB = [arB.get("st0", [128, TT], BF16), arB.get("st1", [128, TT], BF16)]
        sqB = rt1

        srow_free = [0, 0]

        def dequant(wm, wcodes, wscales, oc, par):
            off, hit = cache_slot((wcodes.tensor.name, oc))
            if hit:
                return cache_read(wm, par, off)
            for h in range(4):
                b = wm["hidx"] % 2
                wm["hidx"] += 1
                sem = "w%d" % b
                E("scalar", "dma_start",
                  out=cstI[b][:],
                  in_=wcodes[:, (oc * 4 + h) * 1024:(oc * 4 + h + 1) * 1024],
                  w=[("se", wm["wst_free"][b])], i=[(sem, 16)])
                E("scalar", "activation",
                  out=wm["wbf"][par][:].rearrange("p (kc o) -> p kc o", o=128)[:, :, h * 32:(h + 1) * 32],
                  in_=cstI[b][:].rearrange("p (kc o) -> p kc o", o=32),
                  func=AF.Identity, bias=c_neg4[:], scale=0.5,
                  w=[(sem, CNT[sem]), ("pe", wm["wbf_free"][par]),
                     ("ve", wm.setdefault("sc_done", [0, 0])[par]),
                     ("wc%d" % par, wm.setdefault("wc_free", [0, 0])[par])],
                  i=[("se", 1)])
                wm["wst_free"][b] = CNT["se"]
            for pc in range(8):
                sp = pc % 2
                E("scalar", "dma_start",
                  out=srowf[0][:].rearrange("a (g o) -> a g o", o=128),
                  in_=wscales[pc * 4:(pc + 1) * 4, oc * 128:(oc + 1) * 128]
                      .unsqueeze(0),
                  w=[("ve", srow_free[0])], i=[("sr", 16)])
                E("vector", "tensor_copy", out=srow[sp][:], in_=srowf[0][:],
                  w=[("sr", CNT["sr"]),
                     ("pe", wm.setdefault("srd", [0, 0])[sp])], i=[("ve", 1)])
                srow_free[0] = CNT["ve"]
                bank = 2 + sp
                t_mm(PB[bank][:, 0:TT], c_onesr[:], srow[sp][:], start=True,
                     stop=True, w=[cur("ve"),
                                   ("ve", wm.setdefault("sb_done", [0, 0])[sp])],
                     i=[("pe", 1)])
                wm["srd"][sp] = CNT["pe"]
                E("vector", "tensor_copy", out=sbb[sp][:], in_=PB[bank][:, 0:TT],
                  w=[cur("pe")], i=[("ve", 1)])
                E("vector", "tensor_mul",
                  out=wm["wbf"][par][:, pc * TT:(pc + 1) * TT],
                  in0=wm["wbf"][par][:, pc * TT:(pc + 1) * TT], in1=sbb[sp][:],
                  w=[cur("se")], i=[("ve", 1)])
                wm["sb_done"][sp] = CNT["ve"]
            wm["sc_done"][par] = CNT["ve"]
            cache_write.off = off
            cache_write(wm["wbf"][par], par, cur("ve"))
            wm.setdefault("wc_free", [0, 0])[par] = CNT["wc%d" % par]
            return cur("ve")

        def rope_tile(stg):
            E("vector", "tensor_mul", out=rt1[0:64, :], in0=stg[0:64, :],
              in1=ropA[0:64, :], w=[cur("vsq")], i=[("ve", 1)])
            E("vector", "tensor_mul", out=rt2[0:64, :], in0=stg[64:128, :],
              in1=ropA[64:128, :], i=[("ve", 1)])
            E("vector", "tensor_mul", out=rt1[64:128, :], in0=stg[64:128, :],
              in1=ropB[64:128, :], i=[("ve", 1)])
            E("vector", "tensor_mul", out=rt2[64:128, :], in0=stg[0:64, :],
              in1=ropB[0:64, :], i=[("ve", 1)])
            E("vector", "tensor_sub", out=stg[0:64, :], in0=rt1[0:64, :],
              in1=rt2[0:64, :], i=[("ve", 1)])
            E("vector", "tensor_add", out=stg[64:128, :], in0=rt1[64:128, :],
              in1=rt2[64:128, :], i=[("ve", 1)])

        BWB = D + 2 * KVH * HD
        bstg_free = [0, 0]
        bpb_free = [0, 0]
        for tt in range(NTT):
            E("sync", "dma_start", out=ropA[:], in_=ropeA[:, tt * TT:(tt + 1) * TT],
              w=[cur("ve")], i=[("ext", 16)])
            E("sync", "dma_start", out=ropB[:], in_=ropeB[:, tt * TT:(tt + 1) * TT],
              i=[("ext", 16)])
            E("vector", None, w=[cur("ext")])
            xn_ready = ln_into_xn(tt)
            vview = agin_b[D + KVH * HD: D + 2 * KVH * HD, :]
            specs = [("k", kcod, kscl, KVH), ("v", vcod, vscl, KVH),
                     ("q", qcod, qscl, 32)]
            n = 0
            for kind, wc, ws, cnt_ in specs:
                for h in range(cnt_):
                    par = n % 2
                    cvt_ready = dequant(wmB, wc, ws, h, par)
                    proj_matmuls(wmB, par, par, cvt_ready, xn_ready,
                                 bpb_free[par], last_stop=True, i=[("pe", 1)])
                    stg = stgB[par]
                    ssem = "st%d" % par
                    E("vector", "tensor_copy", out=stg[:], in_=PB[par][:, :],
                      w=[cur("pe"), (ssem, bstg_free[par])], i=[("ve", 1)])
                    bpb_free[par] = CNT["ve"]
                    if kind == "k":
                        rope_tile(stg)
                        E("sync", "dma_start",
                          out=agin_b[D + h * 128: D + (h + 1) * 128,
                                     tt * TT:(tt + 1) * TT],
                          in_=stg[:], w=[cur("ve")], i=[(ssem, 16)])
                        bstg_free[par] = CNT[ssem]
                    elif kind == "v":
                        v_transpose_store(stg, sqB, vview, tt, h, None)
                        bstg_free[par] = 0
                    else:
                        rope_tile(stg)
                        E("sync", "dma_start",
                          out=agin_b[h * 128:(h + 1) * 128, tt * TT:(tt + 1) * TT],
                          in_=stg[:], w=[cur("ve")], i=[(ssem, 16)])
                        bstg_free[par] = CNT[ssem]
                    n += 1
        E("gpsimd", "collective_compute", "AllGather", ALU.bypass,
          [list(range(NCORES))], [agin_b.opt()], [agout_b.opt()],
          w=[cur("st0"), cur("st1"), cur("vsq")], i=[("cc", 1)])
        barrier()
        extract_b()
        emit_flash(dict(alibi=False, WB=WB_B, qloc=q_locb, kloc=k_locb,
                        vloc=v_locb, ao_in=aoin_b, negaq=None, kv_shared=True))
        E("gpsimd", "collective_compute", "AllGather", ALU.bypass,
          [list(range(NCORES))], [aoin_b.opt()], [aoout_b.opt()],
          w=[cur("ao0"), cur("ao1")], i=[("cc", 1)])
        barrier()
        extract_ao(aoout_b, ao_loc_b)
        arO = Arena("Ob_")
        wmO = make_wmach(arO, "")
        cstO = [wmO["wst"][0].ap().bitcast(I32), wmO["wst"][1].ap().bitcast(I32)]
        srowOf = [arO.get("sf0", [1, TT], F32), arO.get("sf1", [1, TT], F32)]
        srowO = [arO.get("sr0", [1, TT], BF16), arO.get("sr1", [1, TT], BF16)]
        sbO = [arO.get("sb0", [128, TT], BF16), arO.get("sb1", [128, TT], BF16)]
        stgO = [arO.get("st0", [128, TT], BF16), arO.get("st1", [128, TT], BF16)]
        srowO_free = [0, 0]

        def dequantO(oc, par, wm=wmO):
            off, hit = cache_slot(("ocod", oc))
            if hit:
                return cache_read(wm, par, off)
            for h in range(4):
                b = wm["hidx"] % 2
                wm["hidx"] += 1
                sem = "w%d" % b
                E("scalar", "dma_start",
                  out=cstO[b][:],
                  in_=ocod[:, (oc * 4 + h) * 1024:(oc * 4 + h + 1) * 1024],
                  w=[("se", wm["wst_free"][b])], i=[(sem, 16)])
                E("scalar", "activation",
                  out=wm["wbf"][par][:].rearrange("p (kc o) -> p kc o", o=128)[:, :, h * 32:(h + 1) * 32],
                  in_=cstO[b][:].rearrange("p (kc o) -> p kc o", o=32),
                  func=AF.Identity, bias=c_neg4[:], scale=0.5,
                  w=[(sem, CNT[sem]), ("pe", wm["wbf_free"][par]),
                     ("ve", wm.setdefault("sc_done", [0, 0])[par]),
                     ("wc%d" % par, wm.setdefault("wc_free", [0, 0])[par])],
                  i=[("se", 1)])
                wm["wst_free"][b] = CNT["se"]
            for pc in range(8):
                sp = pc % 2
                E("scalar", "dma_start",
                  out=srowOf[0][:].rearrange("a (g o) -> a g o", o=128),
                  in_=oscl[pc * 4:(pc + 1) * 4, oc * 128:(oc + 1) * 128]
                      .unsqueeze(0),
                  w=[("ve", srowO_free[0])], i=[("sr", 16)])
                E("vector", "tensor_copy", out=srowO[sp][:], in_=srowOf[0][:],
                  w=[("sr", CNT["sr"]),
                     ("pe", wm.setdefault("srd", [0, 0])[sp])], i=[("ve", 1)])
                srowO_free[0] = CNT["ve"]
                bank = 2 + sp
                t_mm(PB[bank][:, 0:TT], c_onesr[:], srowO[sp][:], start=True,
                     stop=True, w=[cur("ve"),
                                   ("ve", wm.setdefault("sb_done", [0, 0])[sp])],
                     i=[("pe", 1)])
                wm["srd"][sp] = CNT["pe"]
                E("vector", "tensor_copy", out=sbO[sp][:], in_=PB[bank][:, 0:TT],
                  w=[cur("pe")], i=[("ve", 1)])
                E("vector", "tensor_mul",
                  out=wm["wbf"][par][:, pc * TT:(pc + 1) * TT],
                  in0=wm["wbf"][par][:, pc * TT:(pc + 1) * TT], in1=sbO[sp][:],
                  w=[cur("se")], i=[("ve", 1)])
                wm["sb_done"][sp] = CNT["ve"]
            wm["sc_done"][par] = CNT["ve"]
            cache_write.off = off
            cache_write(wm["wbf"][par], par, cur("ve"))
            wm.setdefault("wc_free", [0, 0])[par] = CNT["wc%d" % par]
            return cur("ve")

        o_proj(ao_loc_b, w_src=None, bias=None, dequant=dequantO, wm=wmO,
               stg2=stgO)
        barrier()

    if PHASES >= 1:
        phase_attn_a()
    if PHASES >= 2:
        phase_ffn(0)
    if PHASES >= 3:
        phase_attn_b()
    if PHASES >= 4:
        phase_ffn(1)

    E("sync", "dma_start", out=xout.rearrange("(c p) t -> p c t", p=128),
      in_=x_sb[:].rearrange("p (c t) -> p c t", t=TPC),
      w=[cur("ve"), cur("pe"), cur("se")], i=[("fin", 16)])
    E("sync", None, w=[("fin", 16)])
    E("gpsimd", None, w=[("fin", 16)])

    with nc.Block() as block:
        for eng in ENGS:
            getattr(block, eng)(lambda h, _e=eng: [f(h) for f in PLAN[_e]])
    stack.close()
    return nc



def _alibi_slopes():
    m = (np.arange(1, 33, dtype=np.float64)) * 0.25
    return 1.0 / (2.0 ** m)


def _ilv(wt):
    Dk, O = wt.shape
    a = wt.reshape(Dk // 128, 128, O // 32, 32)
    return np.ascontiguousarray(a.transpose(1, 2, 0, 3).reshape(128, -1))


def _host_stage(inputs):
    f32 = np.float32
    x = np.asarray(inputs["L0_attn_a_x"], dtype=f32).reshape(S, D)
    slopes = _alibi_slopes()

    stg = {}
    stg["wq_t"] = _ilv(np.asarray(inputs["L0_attn_a_w_qkv"], f32).T)
    stg["bqkv"] = np.ascontiguousarray(
        np.asarray(inputs["L0_attn_a_b_qkv"], f32).reshape(96, 128).T)
    perm = np.empty(D, dtype=np.int64)
    for r in range(8):
        for s in range(4):
            h = 8 * s + r
            perm[(r * 4 + s) * 128:(r * 4 + s + 1) * 128] = np.arange(h * 128, (h + 1) * 128)
    stg["wo_t"] = _ilv(np.asarray(inputs["L0_attn_a_w_out"], f32).T[perm, :])
    stg["bout"] = np.ascontiguousarray(
        np.asarray(inputs["L0_attn_a_b_out"], f32).reshape(32, 128).T)
    stg["wb_t"] = _ilv(np.asarray(inputs["L0_ffn_W_base"], f32).T)
    stg["bbase"] = np.ascontiguousarray(
        np.asarray(inputs["L0_ffn_b_base"], f32).reshape(32, 128).T)
    stg["wr_t"] = np.ascontiguousarray(np.asarray(inputs["L0_ffn_W_router"], f32).T)
    stg["wa_t"] = _ilv(np.asarray(inputs["L0_ffn_lora_A"], f32).reshape(128, D).T)
    stg["wb2"] = np.ascontiguousarray(
        np.asarray(inputs["L0_ffn_lora_B"], f32).transpose(0, 2, 1).reshape(128, D))
    stg["qcod"] = _ilv(np.asarray(inputs["L0_attn_b_q_codes"], np.int32))
    stg["kcod"] = _ilv(np.asarray(inputs["L0_attn_b_k_codes"], np.int32))
    stg["vcod"] = _ilv(np.asarray(inputs["L0_attn_b_v_codes"], np.int32))
    stg["qscl"] = np.ascontiguousarray(np.asarray(inputs["L0_attn_b_q_scales"], f32))
    stg["kscl"] = np.ascontiguousarray(np.asarray(inputs["L0_attn_b_k_scales"], f32))
    stg["vscl"] = np.ascontiguousarray(np.asarray(inputs["L0_attn_b_v_scales"], f32))
    permb = np.empty(D, dtype=np.int64)
    for r in range(8):
        for s in range(4):
            h = 4 * r + s
            permb[(r * 4 + s) * 128:(r * 4 + s + 1) * 128] = np.arange(h * 128, (h + 1) * 128)
    stg["ocod"] = _ilv(np.asarray(inputs["L0_attn_b_o_codes"], np.int32)[permb, :])
    gperm = permb[::128] // 128
    stg["oscl"] = np.ascontiguousarray(
        np.asarray(inputs["L0_attn_b_o_scales"], f32)[gperm, :])

    stg["ident"] = np.eye(128, dtype=BF)
    stg["identf"] = np.eye(128, dtype=f32)
    stg["onesr"] = np.ones((1, 128), BF)
    stg["onesrf"] = np.ones((1, 128), f32)
    stg["onesc"] = np.ones((128, 1), BF)
    stg["ones512"] = np.ones((1, TT), BF)
    tri = np.zeros((128, 4 * TT), f32)
    for kc in range(4):
        for kp in range(128):
            tri[kp, kc * TT:(kc + 1) * TT] = np.where(
                np.arange(TT) >= (kc * 128 + kp), 3e38, -1e9)
    stg["tri"] = tri.astype(BF)
    stg["epsap"] = np.full((1, 1), EPS, f32)
    stg["iota128f"] = np.arange(128, dtype=f32).reshape(1, 128)
    exp_ = np.zeros((8, 128), BF)
    for e in range(8):
        exp_[e, e * 16:(e + 1) * 16] = 1.0
    stg["expand"] = exp_

    per_core = []
    for c in range(NCORES):
        rows = np.concatenate([np.arange(c * 512, (c + 1) * 512),
                               np.arange((15 - c) * 512, (16 - c) * 512)])
        d = dict(stg)
        d["x0"] = np.ascontiguousarray(x[rows, :].T)
        pos = rows.astype(np.float64)
        inv = 1.0 / (10000.0 ** (np.arange(64, dtype=np.float64) / 64.0))
        ang = pos[None, :] * inv[:, None]
        ra = np.zeros((128, TPC), np.float64)
        rb = np.zeros((128, TPC), np.float64)
        ra[0:64] = np.cos(ang)
        ra[64:128] = np.sin(ang)
        rb[0:64] = np.sin(ang)
        rb[64:128] = np.cos(ang)
        d["ropeA"] = ra.astype(BF)
        d["ropeB"] = rb.astype(BF)
        sl = np.zeros((1, 256), f32)
        ba = np.zeros((1, 256), f32)
        nq = np.zeros((4, S), np.float64)
        for s in range(4):
            h = 8 * s + c
            sp = slopes[h]
            sl[0, s * 64:(s + 1) * 64] = sp
            ba[0, s * 64:(s + 1) * 64] = sp * (128.0 * np.arange(64) - (S - 1))
            nq[s, :] = -SQHD * sp * (np.arange(S, dtype=np.float64) - (S - 1))
        d["sloper"] = sl
        d["baser"] = ba
        d["negaq"] = nq.astype(BF)
        per_core.append(d)
    return per_core


def kernel(**inputs):
    per_core = _host_stage(inputs)
    nc = build_program()
    res = run_bass_kernel_spmd(nc, per_core, core_ids=list(range(NCORES)))
    out = np.zeros((S, D), dtype=np.float32)
    for c in range(NCORES):
        r = np.asarray(res.results[c]["xout"])
        rows = np.concatenate([np.arange(c * 512, (c + 1) * 512),
                               np.arange((15 - c) * 512, (16 - c) * 512)])
        out[rows, :] = r.T
    return out.reshape(1, S, D)
